# version 10
# baseline (speedup 1.0000x reference)
import numpy as np
from contextlib import ExitStack

import concourse.bass as bass
import concourse.mybir as mybir
from concourse.bass_utils import run_bass_kernel_spmd

F32 = mybir.dt.float32
BF16 = mybir.dt.bfloat16
AF = mybir.ActivationFunctionType
ALU = mybir.AluOpType
AX = mybir.AxisListType

NCORES = 8
D = 4096
NCH = 32
TLOC = 1024
NT = 8
NKV = 8
GQ = 4
NQH = 32
DH = 128
EPS = 1e-5
INV_SQRT_DH = 0.08838834764831845
NEG_HALF_LN128 = -2.4260151319598084
NEG_BIG = -1.0e9

C_INVSQ, C_NEGHALF, C_EPS, C_HALF, C_MHLN128, C_NEG1, C_NEGM = range(7)
CONSTS = [INV_SQRT_DH, -0.5, EPS, 0.5, NEG_HALF_LN128, -1.0, -40.0]

SUM, MEAN, MSQ, M2, VAR, LNV, RSTD, NEGB, NTMP, NMAX = range(10)


class S:

    def __init__(self, h):
        self.h = h
        self.n = 0

    def inc(self, instr, k=1):
        self.n += k
        instr.then_inc(self.h, k)
        return instr


class DPair:

    def __init__(self, s0, s1):
        self.s = [s0, s1]

    def base(self):
        return (self.s[0].n, self.s[1].n)

    def inc(self, instr, idx):
        return self.s[idx % 2].inc(instr, 16)

    def wait(self, eng, base, idx, per=16):
        p = idx % 2
        eng.wait_ge(self.s[p].h, base[p] + per * (idx // 2 + 1))

    def wait_all(self, eng):
        for s in self.s:
            if s.n:
                eng.wait_ge(s.h, s.n)


class Emitter:

    def __init__(self, default_sem):
        self.ds = default_sem
        self.pending = {}

    def emit(self, instr, sem=None):
        s = sem if sem is not None else self.ds
        s.inc(instr)
        self.pending[s] = s.n
        return instr

    def bar(self, eng):
        for s, c in self.pending.items():
            eng.wait_ge(s.h, c)
        self.pending.clear()


def build_nc(taps=False, upto="full", WO_LEVEL=4):
    nc = bass.Bass("TRN2", target_bir_lowering=False)
    stages = ["setup", "ln1", "proj", "attn", "wo", "ln2", "d0", "full"]
    lim = stages.index(upto)

    x_sh = nc.dram_tensor("x_sh", [TLOC, D], F32, kind="ExternalInput")
    wq_t = nc.dram_tensor("wq_t", [32, 128, NCH, 128], F32, kind="ExternalInput")
    wk_t = nc.dram_tensor("wk_t", [8, 128, NCH, 128], F32, kind="ExternalInput")
    wv_t = nc.dram_tensor("wv_t", [8, 128, NCH, 128], F32, kind="ExternalInput")
    wo_t = nc.dram_tensor("wo_t", [32, 128, NCH, 128], F32, kind="ExternalInput")
    wt_t = nc.dram_tensor("wt_t", [32, 128, NCH, 128], F32, kind="ExternalInput")
    awr = nc.dram_tensor("awr", [128, NCH, 36], F32, kind="ExternalInput")
    bmt = nc.dram_tensor("bmt", [128, D], F32, kind="ExternalInput")
    b_t = nc.dram_tensor("b_t", [128, 32], F32, kind="ExternalInput")
    cos_in = nc.dram_tensor("cos_in", [64, TLOC], F32, kind="ExternalInput")
    sin_in = nc.dram_tensor("sin_in", [64, TLOC], F32, kind="ExternalInput")
    mask_in = nc.dram_tensor("mask_in", [128, 8 * 128], F32, kind="ExternalInput")
    ident_in = nc.dram_tensor("ident_in", [128, 128], F32, kind="ExternalInput")
    consts_in = nc.dram_tensor("consts_in", [128, len(CONSTS)], F32, kind="ExternalInput")
    y = nc.dram_tensor("y", [TLOC, D], F32, kind="ExternalOutput")

    kv_loc = nc.dram_tensor("kv_loc", [2048, 1024], BF16)
    kv_g = nc.dram_tensor("kv_g", [8 * 2048, 1024], BF16)
    lrow_d = nc.dram_tensor("lrow_d", [TLOC], F32)

    tp = {}
    if taps:
        tp["QT"] = nc.dram_tensor("t_QT", [128, NQH * TLOC], BF16, kind="ExternalOutput")
        tp["kvg"] = nc.dram_tensor("t_kvg", [8 * 2048, 1024], BF16, kind="ExternalOutput")
        tp["OT"] = nc.dram_tensor("t_OT", [128, NQH * TLOC], BF16, kind="ExternalOutput")
        tp["x1"] = nc.dram_tensor("t_x1", [TLOC, D], F32, kind="ExternalOutput")
        tp["h2T"] = nc.dram_tensor("t_h2T", [128, NCH * TLOC], BF16, kind="ExternalOutput")

    st = ExitStack()
    with st:
        e = st.enter_context

        d_io = S(e(nc.semaphore("d_io")))
        DW = DPair(S(e(nc.semaphore("dw0"))), S(e(nc.semaphore("dw1"))))
        DX = DPair(S(e(nc.semaphore("dx0"))), S(e(nc.semaphore("dx1"))))
        DK = DPair(S(e(nc.semaphore("dk0"))), S(e(nc.semaphore("dk1"))))
        DV = DPair(S(e(nc.semaphore("dv0"))), S(e(nc.semaphore("dv1"))))
        DS = DPair(S(e(nc.semaphore("ds0"))), S(e(nc.semaphore("ds1"))))
        s_cc = S(e(nc.semaphore("s_cc")))
        p_mm = S(e(nc.semaphore("p_mm")))
        p_sc = S(e(nc.semaphore("p_sc")))
        p_pv = S(e(nc.semaphore("p_pv")))
        p_tp = S(e(nc.semaphore("p_tp")))
        v_ev = S(e(nc.semaphore("v_ev")))
        v_ms = S(e(nc.semaphore("v_ms")))
        v_oe = S(e(nc.semaphore("v_oe")))
        v_oc = S(e(nc.semaphore("v_oc")))
        v_lr = S(e(nc.semaphore("v_lr")))
        d_b1 = S(e(nc.semaphore("d_b1")))
        d_b2 = S(e(nc.semaphore("d_b2")))
        v_st = S(e(nc.semaphore("v_st")))
        DX3 = [S(e(nc.semaphore(f"dx3_{i}"))) for i in range(3)]
        v_gr = S(e(nc.semaphore("v_gr")))
        v_ad = S(e(nc.semaphore("v_ad")))
        sv = S(e(nc.semaphore("sv")))
        a_h = S(e(nc.semaphore("a_h")))
        a_sq = S(e(nc.semaphore("a_sq")))
        a_ex = S(e(nc.semaphore("a_ex")))
        sa = S(e(nc.semaphore("sa")))
        a_lr = S(e(nc.semaphore("a_lr")))
        p_lm = S(e(nc.semaphore("p_lm")))

        EV = Emitter(sv)
        EA = Emitter(sa)

        def vbar(v):
            EV.bar(v)

        def abar(a):
            EA.bar(a)

        consts = e(nc.sbuf_tensor("consts", [128, len(CONSTS)], F32))
        bT = e(nc.sbuf_tensor("bT", [128, 32], F32))
        maskb = e(nc.sbuf_tensor("maskb", [128, 8 * 128], BF16))
        identb = e(nc.sbuf_tensor("identb", [128, 128], BF16))
        identf = e(nc.sbuf_tensor("identf", [128, 128], F32))
        ones_col = e(nc.sbuf_tensor("ones_col", [128, 1], BF16))
        ropeA = ropeB = cosb = sinb = None
        stt = e(nc.sbuf_tensor("stt", [128, 20], F32))
        QT = e(nc.sbuf_tensor("QT", [128, NQH * TLOC], BF16))
        OT = e(nc.sbuf_tensor("OT", [128, NQH * TLOC], BF16))
        hT = OT

        with nc.Block() as blk:

            @blk.sync
            def _(sy):
                d_io.inc(sy.dma_start(out=consts[:, :], in_=consts_in[:, :]), 16)
                d_io.inc(sy.dma_start(out=bT[:, :], in_=b_t[:, :]), 16)
                d_io.inc(sy.dma_start(out=identf[:, :], in_=ident_in[:, :]), 16)
                sy.wait_ge(d_io.h, d_io.n)

            @blk.gpsimd
            def _(g):
                DW.inc(g.dma_start(out=identb[:, :], in_=ident_in[:, :]), 0)
                DW.inc(g.dma_start(out=maskb[:, :], in_=mask_in[:, :]), 0)
                g.wait_ge(DW.s[0].h, DW.s[0].n)

            @blk.vector
            def _(v):
                sv.inc(v.memset(ones_col[:, :], 1.0))


        def ln_phase(src, dest_hT, tag):
            with ExitStack() as ph:
                pe = ph.enter_context
                xbuf = [pe(nc.sbuf_tensor(f"xb{tag}{i}", [128, D], F32)) for i in range(2)]
                xsq = pe(nc.sbuf_tensor(f"xsq{tag}", [128, D], F32))
                hbf = pe(nc.sbuf_tensor(f"hbf{tag}", [128, D], BF16))
                ptp = [pe(nc.psum_tensor(f"ptp{tag}{i}", [128, 512], BF16)) for i in range(2)]
                blk2 = pe(nc.Block())

                b_io, b_ah, b_asq, b_vst, b_ptp, b_vev, b_sv, b_sa = (
                    DX.base(), a_h.n, a_sq.n, v_st.n, p_tp.n, v_ev.n, sv.n, sa.n)

                @blk2.sync
                def _(sy):
                    for t in range(NT):
                        if t >= 2:
                            sy.wait_ge(a_h.h, b_ah + t - 1)
                        DX.inc(sy.dma_start(out=xbuf[t % 2][:, :],
                                            in_=src[t * 128:(t + 1) * 128, :]), t)

                @blk2.scalar
                def _(a):
                    for t in range(NT):
                        o = 10 * (t % 2)
                        DX.wait(a, b_io, t)
                        if t >= 1:
                            a.wait_ge(v_st.h, b_vst + 2 * t - 1)
                        EA.emit(a.square(xsq[:, :], xbuf[t % 2][:, :]), a_sq)
                        a.wait_ge(v_st.h, b_vst + 2 * t + 1)
                        EA.emit(a.activation(stt[:, o + LNV:o + LNV + 1],
                                             stt[:, o + VAR:o + VAR + 1],
                                             AF.Ln, bias=consts[:, C_EPS:C_EPS + 1]))
                        abar(a)
                        EA.emit(a.activation(stt[:, o + RSTD:o + RSTD + 1],
                                             stt[:, o + LNV:o + LNV + 1],
                                             AF.Exp,
                                             scale=consts[:, C_NEGHALF:C_NEGHALF + 1]))
                        abar(a)
                        a.wait_ge(v_st.h, b_vst + 2 * t + 2)
                        if t >= 1:
                            a.wait_ge(p_tp.h, b_ptp + 8 * t)
                        EA.emit(a.activation(hbf[:, :], xbuf[t % 2][:, :], AF.Identity,
                                             bias=stt[:, o + NEGB:o + NEGB + 1],
                                             scale=stt[:, o + RSTD:o + RSTD + 1]), a_h)

                def emit_evac_group(v, t, g):
                    gi = t * 8 + g
                    v.wait_ge(p_tp.h, b_ptp + gi + 1)
                    dst = bass.AP(dest_hT, (g * 4) * TLOC + t * 128,
                                  [[NCH * TLOC, 128], [TLOC, 4], [1, 128]])
                    src_ap = ptp[gi % 2][:, :].rearrange("p (j t) -> p j t", j=4)
                    EV.emit(v.tensor_copy(dst, src_ap), v_ev)

                @blk2.vector
                def _(v):
                    for t in range(NT + 1):
                        if t >= 1:
                            for g in range(8):
                                emit_evac_group(v, t - 1, g)
                        if t >= NT:
                            break
                        DX.wait(v, b_io, t)
                        o = 10 * (t % 2)
                        if t >= 2:
                            v.wait_ge(a_h.h, b_ah + t - 1)
                        EV.emit(v.reduce_sum(stt[:, o + SUM:o + SUM + 1],
                                             xbuf[t % 2][:, :], AX.X))
                        vbar(v)
                        EV.emit(v.tensor_scalar_mul(stt[:, o + MEAN:o + MEAN + 1],
                                                    stt[:, o + SUM:o + SUM + 1],
                                                    1.0 / D))
                        v.wait_ge(a_sq.h, b_asq + t + 1)
                        EV.emit(v.reduce_sum(stt[:, o + MSQ:o + MSQ + 1], xsq[:, :],
                                             AX.X))
                        vbar(v)
                        EV.emit(v.tensor_scalar_mul(stt[:, o + MSQ:o + MSQ + 1],
                                                    stt[:, o + MSQ:o + MSQ + 1],
                                                    1.0 / D))
                        EV.emit(v.tensor_tensor(stt[:, o + M2:o + M2 + 1],
                                                stt[:, o + MEAN:o + MEAN + 1],
                                                stt[:, o + MEAN:o + MEAN + 1],
                                                ALU.mult))
                        vbar(v)
                        EV.emit(v.tensor_tensor(stt[:, o + VAR:o + VAR + 1],
                                                stt[:, o + MSQ:o + MSQ + 1],
                                                stt[:, o + M2:o + M2 + 1],
                                                ALU.subtract), v_st)
                        v.wait_ge(sa.h, b_sa + 2 * (t + 1))
                        EV.emit(v.scalar_tensor_tensor(stt[:, o + NEGB:o + NEGB + 1],
                                                       stt[:, o + MEAN:o + MEAN + 1],
                                                       -1.0,
                                                       stt[:, o + RSTD:o + RSTD + 1],
                                                       ALU.mult, ALU.mult), v_st)

                @blk2.tensor
                def _(t_):
                    for t in range(NT):
                        t_.wait_ge(a_h.h, b_ah + t + 1)
                        for g in range(8):
                            gi = t * 8 + g
                            if gi >= 2:
                                t_.wait_ge(v_ev.h, b_vev + gi - 1)
                            for j in range(4):
                                ic = g * 4 + j
                                i = t_.transpose(ptp[gi % 2][:, j * 128:(j + 1) * 128],
                                                 hbf[:, ic * 128:(ic + 1) * 128],
                                                 identb[:, :])
                            p_tp.inc(i)

        def proj_loop(blk2, wdram, n_mc, wbuf, ppr, rhs_hT, post_sem, post_per_mc,
                      g_hook=None):
            b_dw, b_pmm, b_post = DW.base(), p_mm.n, post_sem.n

            @blk2.gpsimd
            def _(g):
                for mc in range(n_mc):
                    if g_hook is not None:
                        g_hook(g, mc)
                    if b_pmm + mc - 1 > 0:
                        g.wait_ge(p_mm.h, b_pmm + max(0, mc - 1))
                    src = bass.AP(wdram, mc * 128 * NCH * 128,
                                  [[NCH * 128, 128], [1, NCH * 128]])
                    DW.inc(g.dma_start(out=wbuf[mc % 2][:, :], in_=src), mc)

            @blk2.tensor
            def _(t_):
                for mc in range(n_mc):
                    DW.wait(t_, b_dw, mc)
                    pv = b_post + post_per_mc * max(0, mc - 1)
                    if pv > 0:
                        t_.wait_ge(post_sem.h, pv)
                    i = None
                    for ic in range(NCH):
                        for sg in range(2):
                            i = t_.matmul(
                                ppr[mc % 2][:, sg * 512:(sg + 1) * 512],
                                wbuf[mc % 2][:, ic * 128:(ic + 1) * 128],
                                rhs_hT[:, ic * TLOC + sg * 512:ic * TLOC + (sg + 1) * 512],
                                start=(ic == 0), stop=(ic == NCH - 1))
                    p_mm.inc(i)

            return b_pmm

        def rope_post(v, psum, dest, dest_off, row_len):
            vbar(v)
            EV.emit(v.tensor_tensor(ropeA[:, :], psum[0:64, :], cosb[:, :], ALU.mult))
            EV.emit(v.tensor_tensor(ropeB[:, :], psum[64:128, :], sinb[:, :], ALU.mult))
            vbar(v)
            d0 = bass.AP(dest, dest_off, [[row_len, 64], [1, TLOC]])
            EV.emit(v.tensor_tensor(d0, ropeA[:, :], ropeB[:, :], ALU.subtract))
            vbar(v)
            EV.emit(v.tensor_tensor(ropeA[:, :], psum[0:64, :], sinb[:, :], ALU.mult))
            EV.emit(v.tensor_tensor(ropeB[:, :], psum[64:128, :], cosb[:, :], ALU.mult))
            vbar(v)
            d1 = bass.AP(dest, 64 * row_len + dest_off, [[row_len, 64], [1, TLOC]])
            EV.emit(v.tensor_tensor(d1, ropeA[:, :], ropeB[:, :], ALU.add), v_ev)

        if lim >= 1:
            ln_phase(x_sh, hT, 'a')

        if lim < 2:
            return nc
        with ExitStack() as ph:
            pe = ph.enter_context
            ropeA = pe(nc.sbuf_tensor("ropeA", [64, TLOC], F32))
            ropeB = pe(nc.sbuf_tensor("ropeB", [64, TLOC], F32))
            cosb = pe(nc.sbuf_tensor("cosb", [64, TLOC], F32))
            sinb = pe(nc.sbuf_tensor("sinb", [64, TLOC], F32))
            wbuf = [pe(nc.sbuf_tensor(f"wb{i}", [128, NCH * 128], BF16))
                    for i in range(2)]
            ppr = [pe(nc.psum_tensor(f"ppr{i}", [128, TLOC], F32)) for i in range(2)]
            ptpa = [pe(nc.psum_tensor(f"ptpav{i}", [128, 512], BF16)) for i in range(2)]
            ktst = [pe(nc.sbuf_tensor(f"ktst{i}", [128, TLOC], BF16)) for i in range(2)]
            vstage = [pe(nc.sbuf_tensor(f"vstage{i}", [128, TLOC], BF16))
                      for i in range(2)]
            vloc = pe(nc.sbuf_tensor("vloc", [128, 8 * TLOC], BF16))
            blk2 = pe(nc.Block())

            b_vevK, b_dsK = v_ev.n, DS.base()
            b_pmmK = proj_loop(blk2, wk_t, 8, wbuf, ppr, hT, v_ev, 1)

            @blk2.sync
            def _(sy):
                d_io.inc(sy.dma_start(out=cosb[:, :], in_=cos_in[:, :]), 16)
                d_io.inc(sy.dma_start(out=sinb[:, :], in_=sin_in[:, :]), 16)
            d_cs = d_io.n

            @blk2.vector
            def _(v):
                v.wait_ge(d_io.h, d_cs)
                for mc in range(8):
                    v.wait_ge(p_mm.h, b_pmmK + mc + 1)
                    if mc >= 2:
                        DS.wait(v, b_dsK, mc - 2)
                    rope_post(v, ppr[mc % 2], ktst[mc % 2], 0, TLOC)

            @blk2.sync
            def _(sy):
                for mc in range(8):
                    sy.wait_ge(v_ev.h, b_vevK + mc + 1)
                    DS.inc(sy.dma_start(out=kv_loc[mc * 128:(mc + 1) * 128, :],
                                        in_=ktst[mc % 2][:, :]), mc)

            b_dwV, b_pmmV, b_vevV, b_tpV = DW.base(), p_mm.n, v_ev.n, p_tp.n

            def _vgrp(vh, g2):
                return 3 * (vh - 1) + g2 + 2

            @blk2.gpsimd
            def _(g):
                for mc in range(8):
                    g.wait_ge(p_mm.h, b_pmmV + max(0, mc - 1))
                    src = bass.AP(wv_t, mc * 128 * NCH * 128,
                                  [[NCH * 128, 128], [1, NCH * 128]])
                    DW.inc(g.dma_start(out=wbuf[mc % 2][:, :], in_=src), mc)

            @blk2.tensor
            def _(t_):
                for mc in range(9):
                    if mc < 8:
                        DW.wait(t_, b_dwV, mc)
                        t_.wait_ge(v_ev.h, b_vevV + 3 * max(0, mc - 1))
                        i = None
                        for ic in range(NCH):
                            for sg in range(2):
                                i = t_.matmul(
                                    ppr[mc % 2][:, sg * 512:(sg + 1) * 512],
                                    wbuf[mc % 2][:, ic * 128:(ic + 1) * 128],
                                    hT[:, ic * TLOC + sg * 512:ic * TLOC + (sg + 1) * 512],
                                    start=(ic == 0), stop=(ic == NCH - 1))
                        p_mm.inc(i)
                    if mc >= 1:
                        vh = mc - 1
                        t_.wait_ge(v_ev.h, b_vevV + 3 * vh + 1)
                        for g2 in range(2):
                            if 2 * vh + g2 >= 2:
                                t_.wait_ge(v_ev.h, b_vevV + _vgrp(vh, g2))
                            i = None
                            for j in range(4):
                                tt = g2 * 4 + j
                                i = t_.transpose(ptpa[g2][:, j * 128:(j + 1) * 128],
                                                 vstage[vh % 2][:, tt * 128:(tt + 1) * 128],
                                                 identb[:, :])
                            p_tp.inc(i)

            @blk2.vector
            def _(v):
                for mc in range(8):
                    v.wait_ge(p_mm.h, b_pmmV + mc + 1)
                    if mc >= 2:
                        v.wait_ge(p_tp.h, b_tpV + 2 * (mc - 1))
                    EV.emit(v.tensor_copy(vstage[mc % 2][:, :], ppr[mc % 2][:, :]),
                            v_ev)
                    for g2 in range(2):
                        v.wait_ge(p_tp.h, b_tpV + 2 * mc + g2 + 1)
                        dst = bass.AP(vloc, (g2 * 4) * TLOC + mc * 128,
                                      [[8 * TLOC, 128], [TLOC, 4], [1, 128]])
                        src_ap = ptpa[g2][:, :].rearrange("p (j t) -> p j t", j=4)
                        EV.emit(v.tensor_copy(dst, src_ap), v_ev)

            @blk2.sync
            def _(sy):
                sy.wait_ge(v_ev.h, b_vevV + 24)
                for t in range(NT):
                    d_io.inc(sy.dma_start(
                        out=kv_loc[1024 + t * 128:1024 + (t + 1) * 128, :],
                        in_=vloc[:, t * TLOC:(t + 1) * TLOC]), 16)

            kv_store_cnt = d_io.n

            def _cc_hook(g, mc):
                if mc != 2:
                    return
                g.wait_ge(d_io.h, kv_store_cnt)
                DS.wait_all(g)
                s_cc.inc(g.collective_compute(
                    "AllGather", ALU.bypass,
                    replica_groups=[list(range(NCORES))],
                    ins=[kv_loc.ap().opt()],
                    outs=[kv_g.ap().opt()]))

            b_pmmQ = proj_loop(blk2, wq_t, 32, wbuf, ppr, hT, v_ev, 1,
                               g_hook=_cc_hook)

            @blk2.vector
            def _(v):
                for mc in range(32):
                    v.wait_ge(p_mm.h, b_pmmQ + mc + 1)
                    rope_post(v, ppr[mc % 2], QT, mc * TLOC, NQH * TLOC)

        if taps:
            with nc.Block() as blkt:
                @blkt.sync
                def _(sy):
                    d_io.inc(sy.dma_start(out=tp["QT"][:, :], in_=QT[:, :]), 16)
                    sy.wait_ge(s_cc.h, 1)
                    d_io.inc(sy.dma_start(out=tp["kvg"][:, :], in_=kv_g[:, :]), 16)
                    sy.wait_ge(d_io.h, d_io.n)

        PAIR_UNITS = []
        for B_ in range(NT):
            for b_ in range(8):
                if B_ < 4:
                    PAIR_UNITS.append((B_, b_, 128 * B_, 512))
                    PAIR_UNITS.append((B_, b_, 512, 1024))
                else:
                    PAIR_UNITS.append((B_, b_, 128 * B_, 1024))
        NUP = len(PAIR_UNITS)
        NTRI = NUP // 3
        BAND_END = [16 * (B_ + 1) if B_ < 4 else 64 + 8 * (B_ - 3)
                    for B_ in range(NT)]
        LFIRST, LLAST = {}, {}
        for idx, (B_, b_, a0_, a1_) in enumerate(PAIR_UNITS):
            key = (idx % 3, 0 if a1_ <= 512 else 1)
            LFIRST.setdefault(key, idx)
            LLAST[key] = idx
        MCNT = []
        _mc = 0
        for idx, (B_, b_, a0_, a1_) in enumerate(PAIR_UNITS):
            if a0_ == 128 * B_:
                _mc += 1
            MCNT.append(_mc)
        NMSK = _mc
        DEPTH = 4

        def glm(w):
            return (w // NUP) * NTRI + (w % NUP) // 3 + 1

        if lim < 3:
            return nc
        with ExitStack() as ph:
            pe = ph.enter_context
            kbuf = [pe(nc.sbuf_tensor(f"kb{i}", [128, 8192], BF16)) for i in range(2)]
            vbuf = [pe(nc.sbuf_tensor(f"vb{i}", [128, TLOC], BF16)) for i in range(2)]
            ptb = [pe(nc.sbuf_tensor(f"ptb{i}", [128, 512], BF16)) for i in range(10)]
            lrec = pe(nc.sbuf_tensor("lrec", [1, TLOC], F32))
            ls32 = pe(nc.sbuf_tensor("ls32", [1, TLOC], F32))
            ls64 = pe(nc.sbuf_tensor("ls64", [1, TLOC], F32))
            bcs = [pe(nc.sbuf_tensor(f"bcs{i}", [128, TLOC], F32)) for i in range(2)]
            osb = [pe(nc.sbuf_tensor(f"osb{i}", [128, TLOC], F32)) for i in range(2)]
            pst = [pe(nc.psum_tensor(f"pst{i}", [128, 512], F32)) for i in range(4)]
            pot = pe(nc.psum_tensor("pot", [128, TLOC], F32))
            pl = pe(nc.psum_tensor("pl", [128, TLOC], F32))
            blk2 = pe(nc.Block())

            b_kh, b_vb = DK.base(), DV.base()
            b_sc, b_pv, b_ms, b_ex = p_sc.n, p_pv.n, v_ms.n, a_ex.n
            b_oe, b_oc, b_lr = v_oe.n, v_oc.n, v_lr.n
            b_lm, b_alr = p_lm.n, a_lr.n
            b_db1, b_db2 = d_b1.n, d_b2.n

            def uend(bi):
                return (bi // NT) * NUP + BAND_END[bi % NT]

            @blk2.sync
            def _(sy):
                sy.wait_ge(s_cc.h, 1)
                for h in range(NKV):
                    if h >= 2:
                        sy.wait_ge(p_sc.h, b_sc + (h - 1) * NUP * GQ)
                    for r in range(8):
                        src = bass.AP(kv_g, (r * 2048 + h * 128) * 1024,
                                      [[1024, 128], [1, 1024]])
                        DK.inc(sy.dma_start(out=kbuf[h % 2][:, r * 1024:(r + 1) * 1024],
                                            in_=src), h)
                    for qh in range(GQ):
                        for B in range(NT):
                            bi = (h * GQ + qh) * NT + B
                            if bi % 2 != 0:
                                continue
                            if bi >= 2:
                                sy.wait_ge(p_pv.h, b_pv + uend(bi - 2))
                            dst = vbuf[bi % 2][:, :].rearrange("p (b d) -> p b d", b=8)
                            src = bass.AP(kv_g, (1024 + B * 128) * 1024 + h * 128,
                                          [[1024, 128], [2048 * 1024, 8], [1, 128]])
                            DV.inc(sy.dma_start(out=dst, in_=src), bi)

            @blk2.gpsimd
            def _(g):
                g.wait_ge(s_cc.h, 1)

                def bcast(pair):
                    g.wait_ge(a_lr.h, b_alr + pair + 1)
                    d_b1.inc(g.dma_start(out=lrow_d.ap(), in_=lrec[0:1, :]), 16)
                    g.wait_ge(d_b1.h, d_b1.n)
                    d_b2.inc(g.dma_start(
                        out=bcs[pair % 2][:, :],
                        in_=bass.AP(lrow_d, 0, [[0, 128], [1, TLOC]])), 16)

                for h in range(NKV):
                    for qh in range(GQ):
                        pair = h * GQ + qh
                        for B in range(NT):
                            bi = pair * NT + B
                            if bi % 2 != 1:
                                continue
                            if bi >= 2:
                                g.wait_ge(p_pv.h, b_pv + uend(bi - 2))
                            dst = vbuf[bi % 2][:, :].rearrange("p (b d) -> p b d", b=8)
                            src = bass.AP(kv_g, (1024 + B * 128) * 1024 + h * 128,
                                          [[1024, 128], [2048 * 1024, 8], [1, 128]])
                            DV.inc(g.dma_start(out=dst, in_=src), bi)
                        if pair >= 1:
                            bcast(pair - 1)
                bcast(NKV * GQ - 1)

            @blk2.scalar
            def _(a):
                for h in range(NKV):
                    for qh in range(GQ):
                        pair = h * GQ + qh
                        for idx, (B, b, a0, a1) in enumerate(PAIR_UNITS):
                            u = pair * NUP + idx
                            ln_ = a1 - a0
                            a.wait_ge(p_sc.h, b_sc + u + 1)
                            if u >= 10:
                                a.wait_ge(p_pv.h, b_pv + u - 9)
                                a.wait_ge(p_lm.h, b_lm + glm(u - 10))
                            EA.emit(a.activation(
                                ptb[u % 10][:, 0:ln_], pst[u % 4][:, 0:ln_], AF.Exp,
                                scale=consts[:, C_INVSQ:C_INVSQ + 1],
                                bias=consts[:, C_NEGM:C_NEGM + 1]), a_ex)
                        a.wait_ge(v_lr.h, b_lr + pair + 1)
                        EA.emit(a.activation(ls64[0:1, :], ls32[0:1, :], AF.Ln))
                        abar(a)
                        a.wait_ge(d_b1.h, b_db1 + 16 * pair)
                        EA.emit(a.activation(lrec[0:1, :], ls64[0:1, :], AF.Exp,
                                             scale=consts[0:1, C_NEG1:C_NEG1 + 1]),
                                a_lr)

            @blk2.vector
            def _(v):
                for h in range(NKV):
                    for qh in range(GQ):
                        pair = h * GQ + qh
                        for idx, (B, b, a0, a1) in enumerate(PAIR_UNITS):
                            if a0 != 128 * B:
                                continue
                            u = pair * NUP + idx
                            v.wait_ge(a_ex.h, b_ex + u + 1)
                            EV.emit(v.tensor_tensor(
                                ptb[u % 10][:, 0:128], ptb[u % 10][:, 0:128],
                                maskb[:, b * 128:(b + 1) * 128], ALU.mult), v_ms)
                        v.wait_ge(p_pv.h, b_pv + (pair + 1) * NUP)
                        vbar(v)
                        EV.emit(v.tensor_copy(osb[pair % 2][:, :], pot[:, :]), v_oc)
                        v.wait_ge(p_lm.h, b_lm + (pair + 1) * NTRI)
                        v.wait_ge(a_lr.h, b_alr + pair)
                        EV.emit(v.tensor_copy(ls32[0:1, :], pl[32:33, 0:TLOC]))
                        EV.emit(v.tensor_copy(ls64[0:1, :], pl[64:65, 0:TLOC]))
                        vbar(v)
                        EV.emit(v.tensor_tensor(ls32[0:1, :], pl[0:1, 0:TLOC],
                                                ls32[0:1, :], ALU.add))
                        vbar(v)
                        EV.emit(v.tensor_tensor(ls32[0:1, :], ls32[0:1, :],
                                                ls64[0:1, :], ALU.add), v_lr)
                        if pair >= 1:
                            pv_ = pair - 1
                            v.wait_ge(d_b2.h, b_db2 + 16 * (pv_ + 1))
                            vbar(v)
                            EV.emit(v.tensor_tensor(
                                OT[:, pv_ * TLOC:(pv_ + 1) * TLOC],
                                osb[pv_ % 2][:, :], bcs[pv_ % 2][:, :],
                                ALU.mult), v_oe)
                pv_ = NKV * GQ - 1
                v.wait_ge(d_b2.h, b_db2 + 16 * (pv_ + 1))
                vbar(v)
                EV.emit(v.tensor_tensor(OT[:, pv_ * TLOC:(pv_ + 1) * TLOC],
                                        osb[pv_ % 2][:, :], bcs[pv_ % 2][:, :],
                                        ALU.mult), v_oe)

            @blk2.tensor
            def _(t_):
                from collections import deque
                pend = deque()
                lpend = []

                def emit_pv_unit(u, pair, B, b, a0, a1, idx, bi):
                    if a0 == 128 * B:
                        t_.wait_ge(v_ms.h, b_ms + pair * NMSK + MCNT[idx])
                    else:
                        t_.wait_ge(a_ex.h, b_ex + u + 1)
                    if idx == 0:
                        t_.wait_ge(v_oc.h, b_oc + pair)
                    first = (B == 0 and b == 0)
                    if a1 <= 512:
                        sp = (B == 3 and b == 7)
                    else:
                        sp = (B == 7 and b == 7)
                    i = t_.matmul(pot[:, a0:a1],
                                  vbuf[bi % 2][:, b * 128:(b + 1) * 128],
                                  ptb[u % 10][:, 0:a1 - a0], start=first, stop=sp)
                    p_pv.inc(i)

                def emit_l_triple():
                    for (u, pair, B, b, a0, a1, idx, bi) in lpend:
                        if a0 == 128 * B:
                            t_.wait_ge(v_ms.h, b_ms + pair * NMSK + MCNT[idx])
                    umax = max(info[0] for info in lpend)
                    t_.wait_ge(a_ex.h, b_ex + umax + 1)
                    if lpend[0][6] == 0:
                        t_.wait_ge(v_lr.h, b_lr + lpend[0][1])
                    i = None
                    for (u, pair, B, b, a0, a1, idx, bi) in lpend:
                        row = idx % 3
                        bank = 0 if a1 <= 512 else 1
                        st = (LFIRST[(row, bank)] == idx)
                        sp = (LLAST[(row, bank)] == idx)
                        i = t_.matmul(pl[32 * row:32 * row + 1, a0:a1],
                                      ones_col[:, :], ptb[u % 10][:, 0:a1 - a0],
                                      start=st, stop=sp,
                                      tile_position=(0, 32 * row))
                    p_lm.inc(i)
                    lpend.clear()

                def flush_one():
                    info = pend.popleft()
                    emit_pv_unit(*info)
                    lpend.append(info)
                    if len(lpend) == 3:
                        emit_l_triple()

                for h in range(NKV):
                    DK.wait(t_, b_kh, h, per=128)
                    for qh in range(GQ):
                        pair = h * GQ + qh
                        for idx, (B, b, a0, a1) in enumerate(PAIR_UNITS):
                            u = pair * NUP + idx
                            bi = pair * NT + B
                            if u >= 4:
                                t_.wait_ge(a_ex.h, b_ex + u - 3)
                            if b == 0 and a0 == 128 * B:
                                DV.wait(t_, b_vb, bi)
                            p_sc.inc(t_.matmul(
                                pst[u % 4][:, 0:a1 - a0],
                                kbuf[h % 2][:, (b * 8 + B) * 128:
                                            (b * 8 + B + 1) * 128],
                                QT[:, pair * TLOC + a0:pair * TLOC + a1],
                                start=True, stop=True))
                            pend.append((u, pair, B, b, a0, a1, idx, bi))
                            if len(pend) > DEPTH:
                                flush_one()
                    while pend:
                        flush_one()

        if taps:
            with nc.Block() as blkt:
                @blkt.sync
                def _(sy):
                    d_io.inc(sy.dma_start(out=tp["OT"][:, :], in_=OT[:, :]), 16)
                    sy.wait_ge(d_io.h, d_io.n)

        def out_proj_phase(wdram, rhs_sb, bias_col, with_lora, res_src, tag,
                           level=4):
            with ExitStack() as ph:
                pe = ph.enter_context
                wbuf2 = [pe(nc.sbuf_tensor(f"wc{tag}{i}", [128, NCH * 128], BF16))
                         for i in range(2)]
                pw = [pe(nc.psum_tensor(f"pw{tag}{i}", [128, TLOC], F32)) for i in range(2)]
                ptpf = [pe(nc.psum_tensor(f"ptpf{tag}{i}", [128, 512], F32))
                        for i in range(2)]
                aT = [pe(nc.sbuf_tensor(f"aT{tag}{i}", [128, TLOC], F32)) for i in range(2)]
                atm = [pe(nc.sbuf_tensor(f"atm{tag}{i}", [128, TLOC], F32))
                       for i in range(2)]
                xres = [pe(nc.sbuf_tensor(f"xr{tag}{i}", [128, TLOC], F32))
                        for i in range(3)]
                blk2 = pe(nc.Block())

                b_dw, b_pmm, b_ah, b_ptp, b_vev, b_ds, b_ad = (
                    DW.base(), p_mm.n, a_h.n, p_tp.n, v_ev.n, DS.base(),
                    v_ad.n)
                b_dx3 = [sx.n for sx in DX3]

                @blk2.gpsimd
                def _(g):
                    for oc in range(32):
                        if oc >= 2:
                            g.wait_ge(p_mm.h, b_pmm + oc - 1)
                        src = bass.AP(wdram, oc * 128 * NCH * 128,
                                      [[NCH * 128, 128], [1, NCH * 128]])
                        DW.inc(g.dma_start(out=wbuf2[oc % 2][:, :], in_=src), oc)

                def issue_xres(q, oc):
                    srcx = res_src[:, oc * 128:(oc + 1) * 128].rearrange(
                        "(a p) o -> p a o", p=128)
                    DX3[oc % 3].inc(q.dma_start(
                        out=xres[oc % 3][:, :].rearrange(
                            "p (a o) -> p a o", a=8), in_=srcx), 16)

                if level >= 4:
                    @blk2.sync
                    def _(sy):
                        for so in range(32):
                            sy.wait_ge(v_ad.h, b_ad + so + 1)
                            dsty = y[:, so * 128:(so + 1) * 128].rearrange(
                                "(a p) o -> p a o", p=128)
                            srcy = atm[so % 2][:, :].rearrange(
                                "p (a o) -> p a o", a=8)
                            DS.inc(sy.dma_start(out=dsty, in_=srcy), so)

                @blk2.tensor
                def _(t_):
                    t_.wait_ge(v_oe.h, v_oe.n)
                    for oc in range(33):
                        if oc < 32:
                            DW.wait(t_, b_dw, oc)
                            if oc >= 2 and level >= 2:
                                t_.wait_ge(a_h.h, b_ah + oc - 1)
                            i = None
                            last_plain = not with_lora
                            for ic in range(NCH):
                                for sg in range(2):
                                    i = t_.matmul(
                                        pw[oc % 2][:, sg * 512:(sg + 1) * 512],
                                        wbuf2[oc % 2][:, ic * 128:(ic + 1) * 128],
                                        rhs_sb[:, ic * TLOC + sg * 512:
                                               ic * TLOC + (sg + 1) * 512],
                                        start=(ic == 0),
                                        stop=(last_plain and ic == NCH - 1))
                            if with_lora:
                                for sg in range(2):
                                    i = t_.matmul(
                                        pw[oc % 2][:, sg * 512:(sg + 1) * 512],
                                        bmt_bf[:, oc * 128:(oc + 1) * 128],
                                        grt_sb[:, sg * 512:(sg + 1) * 512],
                                        start=False, stop=True)
                            p_mm.inc(i)
                        if oc >= 1 and level >= 3:
                            po = oc - 1
                            t_.wait_ge(a_h.h, b_ah + po + 1)
                            for g2 in range(2):
                                if po * 2 + g2 >= 2 and level >= 3:
                                    t_.wait_ge(v_ev.h, b_vev + 2 * po + g2 - 1)
                                i = None
                                for j in range(4):
                                    tt = g2 * 4 + j
                                    i = t_.transpose(
                                        ptpf[g2][:, j * 128:(j + 1) * 128],
                                        aT[po % 2][:, tt * 128:(tt + 1) * 128],
                                        identf[:, :])
                                p_tp.inc(i)

                @blk2.scalar
                def _(a):
                    if level < 2:
                        return
                    if level >= 4:
                        for si in range(2):
                            a.wait_ge(DS.s[si].h, b_ds[si])
                        for j in range(3):
                            issue_xres(a, j)
                    for oc in range(32):
                        a.wait_ge(p_mm.h, b_pmm + oc + 1)
                        if oc >= 2 and level >= 3:
                            a.wait_ge(p_tp.h, b_ptp + 2 * (oc - 1))
                        if bias_col is not None:
                            EA.emit(a.activation(aT[oc % 2][:, :], pw[oc % 2][:, :],
                                                 AF.Identity,
                                                 bias=bias_col[:, oc:oc + 1]), a_h)
                        else:
                            EA.emit(a.activation(aT[oc % 2][:, :], pw[oc % 2][:, :],
                                                 AF.Identity), a_h)
                        if level >= 4 and oc + 3 < 32:
                            a.wait_ge(v_ad.h, b_ad + oc + 1)
                            issue_xres(a, oc + 3)

                @blk2.vector
                def _(v):
                    if level < 3:
                        return
                    for po in range(32):
                        for g2 in range(2):
                            v.wait_ge(p_tp.h, b_ptp + 2 * po + g2 + 1)
                            if po >= 2 and level >= 4:
                                DS.wait(v, b_ds, po - 2)
                            dst = bass.AP(atm[po % 2], (g2 * 4) * 128,
                                          [[TLOC, 128], [128, 4], [1, 128]])
                            src_ap = ptpf[g2][:, :].rearrange("p (j t) -> p j t", j=4)
                            EV.emit(v.tensor_copy(dst, src_ap), v_ev)
                        if level >= 4:
                            v.wait_ge(DX3[po % 3].h,
                                      b_dx3[po % 3] + 16 * (po // 3 + 1))
                            vbar(v)
                            EV.emit(v.tensor_tensor(atm[po % 2][:, :],
                                                    atm[po % 2][:, :],
                                                    xres[po % 3][:, :], ALU.add),
                                    v_ad)

        if lim < 4:
            return nc
        out_proj_phase(wo_t, OT, None, False, x_sh, 'o', level=WO_LEVEL)


        with nc.Block() as blkw:
            @blkw.sync
            def _(sy):
                DS.wait_all(sy)
                if taps:
                    d_io.inc(sy.dma_start(out=tp["x1"][:, :], in_=y[:, :]), 16)
                    sy.wait_ge(d_io.h, d_io.n)

        if lim < 5:
            return nc
        hT2 = QT
        ln_phase(y, hT2, 'c')

        if taps:
            with nc.Block() as blkt:
                @blkt.sync
                def _(sy):
                    d_io.inc(sy.dma_start(out=tp["h2T"][:, :], in_=hT2[:, :]), 16)
                    sy.wait_ge(d_io.h, d_io.n)

        if lim < 6:
            return nc
        bmt_bf = st.enter_context(nc.sbuf_tensor("bmt_bf", [128, D], BF16))
        grt_sb = st.enter_context(nc.sbuf_tensor("grt_sb", [128, TLOC], BF16))
        with ExitStack() as ph:
            pe = ph.enter_context
            awr_bf = pe(nc.sbuf_tensor("awr_bf", [128, NCH * 36], BF16))
            grp = pe(nc.sbuf_tensor("grp", [128, 128], BF16))
            gex = pe(nc.sbuf_tensor("gex", [128, 4], F32))
            gfin = pe(nc.sbuf_tensor("gfin", [128, 4], F32))
            pr = pe(nc.psum_tensor("pr", [128, 512], F32))
            pgt = pe(nc.psum_tensor("pgt", [128, 128], BF16))
            blk2 = pe(nc.Block())

            b_dw, b_pmm, b_gr, b_tp, b_aex = DW.base(), p_mm.n, v_gr.n, p_tp.n, a_ex.n

            @blk2.gpsimd
            def _(g):
                DW.inc(g.dma_start(out=awr_bf[:, :],
                                    in_=awr[:, :, :].rearrange("p a b -> p (a b)")), 0)
                DW.inc(g.dma_start(out=bmt_bf[:, :], in_=bmt[:, :]), 1)

            @blk2.tensor
            def _(t_):
                DW.wait(t_, b_dw, 0)
                DW.wait(t_, b_dw, 1)
                for t in range(NT):
                    if t >= 1:
                        t_.wait_ge(v_gr.h, b_gr + 2 * t)
                    i = None
                    for ic in range(NCH):
                        i = t_.matmul(
                            pr[:, 0:36],
                            hT2[:, ic * TLOC + t * 128:ic * TLOC + (t + 1) * 128],
                            awr_bf[:, ic * 36:(ic + 1) * 36],
                            start=(ic == 0), stop=(ic == NCH - 1))
                    p_mm.inc(i)
                    t_.wait_ge(v_gr.h, b_gr + 2 * t + 2)
                    p_tp.inc(t_.transpose(pgt[:, :], grp[:, :], identb[:, :]))

            @blk2.vector
            def _(v):
                EV.emit(v.memset(grp[:, 32:128], 0.0))
                for t in range(NT):
                    v.wait_ge(p_mm.h, b_pmm + t + 1)
                    vbar(v)
                    EV.emit(v.reduce_max(stt[:, NTMP:NTMP + 1], pr[:, 32:36], AX.X))
                    vbar(v)
                    EV.emit(v.tensor_scalar_mul(stt[:, NTMP:NTMP + 1],
                                                stt[:, NTMP:NTMP + 1], -1.0), v_gr)
                    v.wait_ge(a_ex.h, b_aex + t + 1)
                    EV.emit(v.reduce_sum(stt[:, SUM:SUM + 1], gex[:, :], AX.X))
                    vbar(v)
                    EV.emit(v.reciprocal(stt[:, SUM:SUM + 1], stt[:, SUM:SUM + 1]))
                    vbar(v)
                    EV.emit(v.tensor_scalar(gfin[:, :], gex[:, :],
                                            stt[:, SUM:SUM + 1], 2.0,
                                            ALU.mult, ALU.mult))
                    vbar(v)
                    for ex in range(3):
                        EV.emit(v.tensor_scalar(grp[:, ex * 8:(ex + 1) * 8],
                                                pr[:, ex * 8:(ex + 1) * 8],
                                                gfin[:, ex:ex + 1], 0.0,
                                                ALU.mult, ALU.add))
                    EV.emit(v.tensor_scalar(grp[:, 24:32], pr[:, 24:32],
                                            gfin[:, 3:4], 0.0,
                                            ALU.mult, ALU.add), v_gr)
                    v.wait_ge(p_tp.h, b_tp + t + 1)
                    EV.emit(v.tensor_copy(grt_sb[:, t * 128:(t + 1) * 128],
                                          pgt[:, :]), v_ev)

            @blk2.scalar
            def _(a):
                for t in range(NT):
                    a.wait_ge(v_gr.h, b_gr + 2 * t + 1)
                    EA.emit(a.activation(gex[:, :], pr[:, 32:36], AF.Exp,
                                         bias=stt[:, NTMP:NTMP + 1]), a_ex)

        if lim < 7:
            return nc
        out_proj_phase(wt_t, hT2, bT, True, y, 'f')

        with nc.Block() as blkf:
            @blkf.gpsimd
            def _(g):
                DS.wait_all(g)

            @blkf.sync
            def _(sy):
                DS.wait_all(sy)

    return nc


def _host_inputs(L0_attn_X, Wq, Wk, Wv, Wo, W, b, A, Bm, Wr):
    X = np.ascontiguousarray(np.asarray(L0_attn_X, dtype=np.float32).reshape(8192, D))
    Wq = np.asarray(Wq, np.float32)
    Wk = np.asarray(Wk, np.float32)
    Wv = np.asarray(Wv, np.float32)
    Wo = np.asarray(Wo, np.float32)
    W = np.asarray(W, np.float32)
    b = np.asarray(b, np.float32)
    A = np.asarray(A, np.float32)
    Bm = np.asarray(Bm, np.float32)
    Wr = np.asarray(Wr, np.float32)

    def tile_w(M):
        K, Mo = M.shape
        return np.ascontiguousarray(
            M.reshape(K // 128, 128, Mo // 128, 128).transpose(2, 1, 0, 3))

    AWr = np.concatenate([A.reshape(32, D).T, Wr.T], axis=1)
    awr = np.ascontiguousarray(AWr.reshape(NCH, 128, 36).transpose(1, 0, 2))
    bmt = np.zeros((128, D), np.float32)
    bmt[0:32] = Bm.transpose(0, 2, 1).reshape(32, D)
    b_t = np.ascontiguousarray(b.reshape(32, 128).T)

    consts = np.tile(np.asarray(CONSTS, np.float32)[None, :], (128, 1))
    ident = np.eye(128, dtype=np.float32)

    shared = dict(wq_t=tile_w(Wq), wk_t=tile_w(Wk), wv_t=tile_w(Wv),
                  wo_t=tile_w(Wo), wt_t=tile_w(np.ascontiguousarray(W.T)),
                  awr=awr, bmt=bmt, b_t=b_t, consts_in=consts, ident_in=ident)

    Xr = X.reshape(64, 128, D)
    inv_freq = (10000.0 ** (-np.arange(64, dtype=np.float64) / 64.0))
    p_idx = np.arange(128)
    t_idx = np.arange(128)
    in_maps = []
    for c in range(NCORES):
        m = dict(shared)
        m["x_sh"] = np.ascontiguousarray(Xr[c::8].reshape(TLOC, D))
        pos = (128 * (c + 8 * np.arange(8))[:, None]
               + np.arange(128)[None, :]).reshape(-1)
        ang = inv_freq[:, None] * pos[None, :].astype(np.float64)
        m["cos_in"] = np.cos(ang).astype(np.float32)
        m["sin_in"] = np.sin(ang).astype(np.float32)
        mask = np.zeros((128, 8 * 128), np.float32)
        for bb in range(8):
            diff = p_idx[:, None] - t_idx[None, :]
            mask[:, bb * 128:(bb + 1) * 128] = np.where(
                diff > 128 * (c - bb), 0.0, 1.0)
        m["mask_in"] = mask
        in_maps.append(m)
    return in_maps


def kernel(L0_attn_X, L0_attn_Wq, L0_attn_Wk, L0_attn_Wv, L0_attn_Wo,
           L0_ffn_W, L0_ffn_b, L0_ffn_A, L0_ffn_Bm, L0_ffn_Wr):
    in_maps = _host_inputs(L0_attn_X, L0_attn_Wq, L0_attn_Wk, L0_attn_Wv,
                           L0_attn_Wo, L0_ffn_W, L0_ffn_b, L0_ffn_A,
                           L0_ffn_Bm, L0_ffn_Wr)
    nc = build_nc(taps=False)
    res = run_bass_kernel_spmd(nc, in_maps, core_ids=list(range(NCORES)))
    out = np.empty((64, 128, D), np.float32)
    for c in range(NCORES):
        out[c::8] = res.results[c]["y"].reshape(NT, 128, D)
    return out.reshape(1, 8192, D)


# version 11
# speedup vs baseline: 1.1061x; 1.1061x over previous
import numpy as np
from contextlib import ExitStack

import concourse.bass as bass
import concourse.mybir as mybir
from concourse.bass_utils import run_bass_kernel_spmd

F32 = mybir.dt.float32
BF16 = mybir.dt.bfloat16
AF = mybir.ActivationFunctionType
ALU = mybir.AluOpType
AX = mybir.AxisListType

NCORES = 8
D = 4096
NCH = 32
TLOC = 1024
NT = 8
NKV = 8
GQ = 4
NQH = 32
DH = 128
EPS = 1e-5
INV_SQRT_DH = 0.08838834764831845
NEG_HALF_LN128 = -2.4260151319598084
NEG_BIG = -1.0e9

C_INVSQ, C_NEGHALF, C_EPS, C_HALF, C_MHLN128, C_NEG1, C_NEGM = range(7)
CONSTS = [INV_SQRT_DH, -0.5, EPS, 0.5, NEG_HALF_LN128, -1.0, -40.0]

SUM, MEAN, MSQ, M2, VAR, LNV, RSTD, NEGB, NTMP, NMAX = range(10)


class S:

    def __init__(self, h):
        self.h = h
        self.n = 0

    def inc(self, instr, k=1):
        self.n += k
        instr.then_inc(self.h, k)
        return instr


class DPair:

    def __init__(self, s0, s1):
        self.s = [s0, s1]

    def base(self):
        return (self.s[0].n, self.s[1].n)

    def inc(self, instr, idx):
        return self.s[idx % 2].inc(instr, 16)

    def wait(self, eng, base, idx, per=16):
        p = idx % 2
        eng.wait_ge(self.s[p].h, base[p] + per * (idx // 2 + 1))

    def wait_all(self, eng):
        for s in self.s:
            if s.n:
                eng.wait_ge(s.h, s.n)


class Emitter:

    def __init__(self, default_sem):
        self.ds = default_sem
        self.pending = {}

    def emit(self, instr, sem=None):
        s = sem if sem is not None else self.ds
        s.inc(instr)
        self.pending[s] = s.n
        return instr

    def bar(self, eng):
        for s, c in self.pending.items():
            eng.wait_ge(s.h, c)
        self.pending.clear()


def build_nc(taps=False, upto="full", WO_LEVEL=4):
    nc = bass.Bass("TRN2", target_bir_lowering=False)
    stages = ["setup", "ln1", "proj", "attn", "wo", "ln2", "d0", "full"]
    lim = stages.index(upto)

    x_sh = nc.dram_tensor("x_sh", [TLOC, D], F32, kind="ExternalInput")
    wq_t = nc.dram_tensor("wq_t", [32, 128, NCH, 128], F32, kind="ExternalInput")
    wk_t = nc.dram_tensor("wk_t", [8, 128, NCH, 128], F32, kind="ExternalInput")
    wv_t = nc.dram_tensor("wv_t", [8, 128, NCH, 128], F32, kind="ExternalInput")
    wo_t = nc.dram_tensor("wo_t", [32, 128, NCH, 128], F32, kind="ExternalInput")
    wt_t = nc.dram_tensor("wt_t", [32, 128, NCH, 128], F32, kind="ExternalInput")
    awr = nc.dram_tensor("awr", [128, NCH, 36], F32, kind="ExternalInput")
    bmt = nc.dram_tensor("bmt", [128, D], F32, kind="ExternalInput")
    b_t = nc.dram_tensor("b_t", [128, 32], F32, kind="ExternalInput")
    cos_in = nc.dram_tensor("cos_in", [64, TLOC], F32, kind="ExternalInput")
    sin_in = nc.dram_tensor("sin_in", [64, TLOC], F32, kind="ExternalInput")
    mask_in = nc.dram_tensor("mask_in", [128, 8 * 128], F32, kind="ExternalInput")
    ident_in = nc.dram_tensor("ident_in", [128, 128], F32, kind="ExternalInput")
    consts_in = nc.dram_tensor("consts_in", [128, len(CONSTS)], F32, kind="ExternalInput")
    y = nc.dram_tensor("y", [TLOC, D], F32, kind="ExternalOutput")

    kv_loc = nc.dram_tensor("kv_loc", [2048, 1024], BF16)
    kv_g = nc.dram_tensor("kv_g", [8 * 2048, 1024], BF16)
    lrow_d = nc.dram_tensor("lrow_d", [TLOC], F32)

    tp = {}
    if taps:
        tp["QT"] = nc.dram_tensor("t_QT", [128, NQH * TLOC], BF16, kind="ExternalOutput")
        tp["kvg"] = nc.dram_tensor("t_kvg", [8 * 2048, 1024], BF16, kind="ExternalOutput")
        tp["OT"] = nc.dram_tensor("t_OT", [128, NQH * TLOC], BF16, kind="ExternalOutput")
        tp["x1"] = nc.dram_tensor("t_x1", [TLOC, D], F32, kind="ExternalOutput")
        tp["h2T"] = nc.dram_tensor("t_h2T", [128, NCH * TLOC], BF16, kind="ExternalOutput")

    st = ExitStack()
    with st:
        e = st.enter_context

        d_io = S(e(nc.semaphore("d_io")))
        DW = DPair(S(e(nc.semaphore("dw0"))), S(e(nc.semaphore("dw1"))))
        DX = DPair(S(e(nc.semaphore("dx0"))), S(e(nc.semaphore("dx1"))))
        DK = DPair(S(e(nc.semaphore("dk0"))), S(e(nc.semaphore("dk1"))))
        DV4 = [S(e(nc.semaphore(f"dv{i}"))) for i in range(4)]
        DS = DPair(S(e(nc.semaphore("ds0"))), S(e(nc.semaphore("ds1"))))
        s_cc = S(e(nc.semaphore("s_cc")))
        p_mm = S(e(nc.semaphore("p_mm")))
        p_sc = S(e(nc.semaphore("p_sc")))
        p_pv = S(e(nc.semaphore("p_pv")))
        p_tp = S(e(nc.semaphore("p_tp")))
        v_ev = S(e(nc.semaphore("v_ev")))
        v_ms = S(e(nc.semaphore("v_ms")))
        v_oe = S(e(nc.semaphore("v_oe")))
        v_oc = S(e(nc.semaphore("v_oc")))
        v_lr = S(e(nc.semaphore("v_lr")))
        d_b1 = S(e(nc.semaphore("d_b1")))
        d_b2 = S(e(nc.semaphore("d_b2")))
        v_st = S(e(nc.semaphore("v_st")))
        DX3 = [S(e(nc.semaphore(f"dx3_{i}"))) for i in range(3)]
        v_gr = S(e(nc.semaphore("v_gr")))
        v_ad = S(e(nc.semaphore("v_ad")))
        sv = S(e(nc.semaphore("sv")))
        a_h = S(e(nc.semaphore("a_h")))
        a_sq = S(e(nc.semaphore("a_sq")))
        a_ex = S(e(nc.semaphore("a_ex")))
        sa = S(e(nc.semaphore("sa")))
        a_lr = S(e(nc.semaphore("a_lr")))
        p_lm = S(e(nc.semaphore("p_lm")))

        EV = Emitter(sv)
        EA = Emitter(sa)

        def vbar(v):
            EV.bar(v)

        def abar(a):
            EA.bar(a)

        consts = e(nc.sbuf_tensor("consts", [128, len(CONSTS)], F32))
        bT = e(nc.sbuf_tensor("bT", [128, 32], F32))
        maskb = e(nc.sbuf_tensor("maskb", [128, 8 * 128], BF16))
        identb = e(nc.sbuf_tensor("identb", [128, 128], BF16))
        identf = e(nc.sbuf_tensor("identf", [128, 128], F32))
        ones_col = e(nc.sbuf_tensor("ones_col", [128, 1], BF16))
        ropeA = ropeB = cosb = sinb = None
        stt = e(nc.sbuf_tensor("stt", [128, 20], F32))
        QT = e(nc.sbuf_tensor("QT", [128, NQH * TLOC], BF16))
        OT = e(nc.sbuf_tensor("OT", [128, NQH * TLOC], BF16))
        hT = OT

        with nc.Block() as blk:

            @blk.sync
            def _(sy):
                d_io.inc(sy.dma_start(out=consts[:, :], in_=consts_in[:, :]), 16)
                d_io.inc(sy.dma_start(out=bT[:, :], in_=b_t[:, :]), 16)
                d_io.inc(sy.dma_start(out=identf[:, :], in_=ident_in[:, :]), 16)
                sy.wait_ge(d_io.h, d_io.n)

            @blk.gpsimd
            def _(g):
                DW.inc(g.dma_start(out=identb[:, :], in_=ident_in[:, :]), 0)
                DW.inc(g.dma_start(out=maskb[:, :], in_=mask_in[:, :]), 0)
                g.wait_ge(DW.s[0].h, DW.s[0].n)

            @blk.vector
            def _(v):
                sv.inc(v.memset(ones_col[:, :], 1.0))


        def ln_phase(src, dest_hT, tag):
            with ExitStack() as ph:
                pe = ph.enter_context
                xbuf = [pe(nc.sbuf_tensor(f"xb{tag}{i}", [128, D], F32)) for i in range(2)]
                xsq = pe(nc.sbuf_tensor(f"xsq{tag}", [128, D], F32))
                hbf = pe(nc.sbuf_tensor(f"hbf{tag}", [128, D], BF16))
                ptp = [pe(nc.psum_tensor(f"ptp{tag}{i}", [128, 512], BF16)) for i in range(2)]
                blk2 = pe(nc.Block())

                b_io, b_ah, b_asq, b_vst, b_ptp, b_vev, b_sv, b_sa = (
                    DX.base(), a_h.n, a_sq.n, v_st.n, p_tp.n, v_ev.n, sv.n, sa.n)

                @blk2.sync
                def _(sy):
                    for t in range(NT):
                        if t >= 2:
                            sy.wait_ge(a_h.h, b_ah + t - 1)
                        DX.inc(sy.dma_start(out=xbuf[t % 2][:, :],
                                            in_=src[t * 128:(t + 1) * 128, :]), t)

                @blk2.scalar
                def _(a):
                    for t in range(NT):
                        o = 10 * (t % 2)
                        DX.wait(a, b_io, t)
                        if t >= 1:
                            a.wait_ge(v_st.h, b_vst + 2 * t - 1)
                        EA.emit(a.square(xsq[:, :], xbuf[t % 2][:, :]), a_sq)
                        a.wait_ge(v_st.h, b_vst + 2 * t + 1)
                        EA.emit(a.activation(stt[:, o + LNV:o + LNV + 1],
                                             stt[:, o + VAR:o + VAR + 1],
                                             AF.Ln, bias=consts[:, C_EPS:C_EPS + 1]))
                        abar(a)
                        EA.emit(a.activation(stt[:, o + RSTD:o + RSTD + 1],
                                             stt[:, o + LNV:o + LNV + 1],
                                             AF.Exp,
                                             scale=consts[:, C_NEGHALF:C_NEGHALF + 1]))
                        abar(a)
                        a.wait_ge(v_st.h, b_vst + 2 * t + 2)
                        if t >= 1:
                            a.wait_ge(p_tp.h, b_ptp + 8 * t)
                        EA.emit(a.activation(hbf[:, :], xbuf[t % 2][:, :], AF.Identity,
                                             bias=stt[:, o + NEGB:o + NEGB + 1],
                                             scale=stt[:, o + RSTD:o + RSTD + 1]), a_h)

                def emit_evac_group(v, t, g):
                    gi = t * 8 + g
                    v.wait_ge(p_tp.h, b_ptp + gi + 1)
                    dst = bass.AP(dest_hT, (g * 4) * TLOC + t * 128,
                                  [[NCH * TLOC, 128], [TLOC, 4], [1, 128]])
                    src_ap = ptp[gi % 2][:, :].rearrange("p (j t) -> p j t", j=4)
                    EV.emit(v.tensor_copy(dst, src_ap), v_ev)

                @blk2.vector
                def _(v):
                    for t in range(NT + 1):
                        if t >= 1:
                            for g in range(8):
                                emit_evac_group(v, t - 1, g)
                        if t >= NT:
                            break
                        DX.wait(v, b_io, t)
                        o = 10 * (t % 2)
                        if t >= 2:
                            v.wait_ge(a_h.h, b_ah + t - 1)
                        EV.emit(v.reduce_sum(stt[:, o + SUM:o + SUM + 1],
                                             xbuf[t % 2][:, :], AX.X))
                        vbar(v)
                        EV.emit(v.tensor_scalar_mul(stt[:, o + MEAN:o + MEAN + 1],
                                                    stt[:, o + SUM:o + SUM + 1],
                                                    1.0 / D))
                        v.wait_ge(a_sq.h, b_asq + t + 1)
                        EV.emit(v.reduce_sum(stt[:, o + MSQ:o + MSQ + 1], xsq[:, :],
                                             AX.X))
                        vbar(v)
                        EV.emit(v.tensor_scalar_mul(stt[:, o + MSQ:o + MSQ + 1],
                                                    stt[:, o + MSQ:o + MSQ + 1],
                                                    1.0 / D))
                        EV.emit(v.tensor_tensor(stt[:, o + M2:o + M2 + 1],
                                                stt[:, o + MEAN:o + MEAN + 1],
                                                stt[:, o + MEAN:o + MEAN + 1],
                                                ALU.mult))
                        vbar(v)
                        EV.emit(v.tensor_tensor(stt[:, o + VAR:o + VAR + 1],
                                                stt[:, o + MSQ:o + MSQ + 1],
                                                stt[:, o + M2:o + M2 + 1],
                                                ALU.subtract), v_st)
                        v.wait_ge(sa.h, b_sa + 2 * (t + 1))
                        EV.emit(v.scalar_tensor_tensor(stt[:, o + NEGB:o + NEGB + 1],
                                                       stt[:, o + MEAN:o + MEAN + 1],
                                                       -1.0,
                                                       stt[:, o + RSTD:o + RSTD + 1],
                                                       ALU.mult, ALU.mult), v_st)

                @blk2.tensor
                def _(t_):
                    for t in range(NT):
                        t_.wait_ge(a_h.h, b_ah + t + 1)
                        for g in range(8):
                            gi = t * 8 + g
                            if gi >= 2:
                                t_.wait_ge(v_ev.h, b_vev + gi - 1)
                            for j in range(4):
                                ic = g * 4 + j
                                i = t_.transpose(ptp[gi % 2][:, j * 128:(j + 1) * 128],
                                                 hbf[:, ic * 128:(ic + 1) * 128],
                                                 identb[:, :])
                            p_tp.inc(i)

        def proj_loop(blk2, wdram, n_mc, wbuf, ppr, rhs_hT, post_sem, post_per_mc,
                      g_hook=None):
            b_dw, b_pmm, b_post = DW.base(), p_mm.n, post_sem.n

            @blk2.gpsimd
            def _(g):
                for mc in range(n_mc):
                    if g_hook is not None:
                        g_hook(g, mc)
                    if b_pmm + mc - 1 > 0:
                        g.wait_ge(p_mm.h, b_pmm + max(0, mc - 1))
                    src = bass.AP(wdram, mc * 128 * NCH * 128,
                                  [[NCH * 128, 128], [1, NCH * 128]])
                    DW.inc(g.dma_start(out=wbuf[mc % 2][:, :], in_=src), mc)

            @blk2.tensor
            def _(t_):
                for mc in range(n_mc):
                    DW.wait(t_, b_dw, mc)
                    pv = b_post + post_per_mc * max(0, mc - 1)
                    if pv > 0:
                        t_.wait_ge(post_sem.h, pv)
                    i = None
                    for ic in range(NCH):
                        for sg in range(2):
                            i = t_.matmul(
                                ppr[mc % 2][:, sg * 512:(sg + 1) * 512],
                                wbuf[mc % 2][:, ic * 128:(ic + 1) * 128],
                                rhs_hT[:, ic * TLOC + sg * 512:ic * TLOC + (sg + 1) * 512],
                                start=(ic == 0), stop=(ic == NCH - 1))
                    p_mm.inc(i)

            return b_pmm

        def rope_post(v, psum, dest, dest_off, row_len):
            vbar(v)
            EV.emit(v.tensor_tensor(ropeA[:, :], psum[0:64, :], cosb[:, :], ALU.mult))
            EV.emit(v.tensor_tensor(ropeB[:, :], psum[64:128, :], sinb[:, :], ALU.mult))
            vbar(v)
            d0 = bass.AP(dest, dest_off, [[row_len, 64], [1, TLOC]])
            EV.emit(v.tensor_tensor(d0, ropeA[:, :], ropeB[:, :], ALU.subtract))
            vbar(v)
            EV.emit(v.tensor_tensor(ropeA[:, :], psum[0:64, :], sinb[:, :], ALU.mult))
            EV.emit(v.tensor_tensor(ropeB[:, :], psum[64:128, :], cosb[:, :], ALU.mult))
            vbar(v)
            d1 = bass.AP(dest, 64 * row_len + dest_off, [[row_len, 64], [1, TLOC]])
            EV.emit(v.tensor_tensor(d1, ropeA[:, :], ropeB[:, :], ALU.add), v_ev)

        if lim >= 1:
            ln_phase(x_sh, hT, 'a')

        if lim < 2:
            return nc
        with ExitStack() as ph:
            pe = ph.enter_context
            ropeA = pe(nc.sbuf_tensor("ropeA", [64, TLOC], F32))
            ropeB = pe(nc.sbuf_tensor("ropeB", [64, TLOC], F32))
            cosb = pe(nc.sbuf_tensor("cosb", [64, TLOC], F32))
            sinb = pe(nc.sbuf_tensor("sinb", [64, TLOC], F32))
            wbuf = [pe(nc.sbuf_tensor(f"wb{i}", [128, NCH * 128], BF16))
                    for i in range(2)]
            ppr = [pe(nc.psum_tensor(f"ppr{i}", [128, TLOC], F32)) for i in range(2)]
            ptpa = [pe(nc.psum_tensor(f"ptpav{i}", [128, 512], BF16)) for i in range(2)]
            ktst = [pe(nc.sbuf_tensor(f"ktst{i}", [128, TLOC], BF16)) for i in range(2)]
            vstage = [pe(nc.sbuf_tensor(f"vstage{i}", [128, TLOC], BF16))
                      for i in range(2)]
            vloc = pe(nc.sbuf_tensor("vloc", [128, 8 * TLOC], BF16))
            blk2 = pe(nc.Block())

            b_vevK, b_dsK = v_ev.n, DS.base()
            b_pmmK = proj_loop(blk2, wk_t, 8, wbuf, ppr, hT, v_ev, 1)

            @blk2.sync
            def _(sy):
                d_io.inc(sy.dma_start(out=cosb[:, :], in_=cos_in[:, :]), 16)
                d_io.inc(sy.dma_start(out=sinb[:, :], in_=sin_in[:, :]), 16)
            d_cs = d_io.n

            @blk2.vector
            def _(v):
                v.wait_ge(d_io.h, d_cs)
                for mc in range(8):
                    v.wait_ge(p_mm.h, b_pmmK + mc + 1)
                    if mc >= 2:
                        DS.wait(v, b_dsK, mc - 2)
                    rope_post(v, ppr[mc % 2], ktst[mc % 2], 0, TLOC)

            @blk2.sync
            def _(sy):
                for mc in range(8):
                    sy.wait_ge(v_ev.h, b_vevK + mc + 1)
                    DS.inc(sy.dma_start(out=kv_loc[mc * 128:(mc + 1) * 128, :],
                                        in_=ktst[mc % 2][:, :]), mc)

            b_dwV, b_pmmV, b_vevV, b_tpV = DW.base(), p_mm.n, v_ev.n, p_tp.n

            def _vgrp(vh, g2):
                return 3 * (vh - 1) + g2 + 2

            @blk2.gpsimd
            def _(g):
                for mc in range(8):
                    g.wait_ge(p_mm.h, b_pmmV + max(0, mc - 1))
                    src = bass.AP(wv_t, mc * 128 * NCH * 128,
                                  [[NCH * 128, 128], [1, NCH * 128]])
                    DW.inc(g.dma_start(out=wbuf[mc % 2][:, :], in_=src), mc)

            @blk2.tensor
            def _(t_):
                for mc in range(9):
                    if mc < 8:
                        DW.wait(t_, b_dwV, mc)
                        t_.wait_ge(v_ev.h, b_vevV + 3 * max(0, mc - 1))
                        i = None
                        for ic in range(NCH):
                            for sg in range(2):
                                i = t_.matmul(
                                    ppr[mc % 2][:, sg * 512:(sg + 1) * 512],
                                    wbuf[mc % 2][:, ic * 128:(ic + 1) * 128],
                                    hT[:, ic * TLOC + sg * 512:ic * TLOC + (sg + 1) * 512],
                                    start=(ic == 0), stop=(ic == NCH - 1))
                        p_mm.inc(i)
                    if mc >= 1:
                        vh = mc - 1
                        t_.wait_ge(v_ev.h, b_vevV + 3 * vh + 1)
                        for g2 in range(2):
                            if 2 * vh + g2 >= 2:
                                t_.wait_ge(v_ev.h, b_vevV + _vgrp(vh, g2))
                            i = None
                            for j in range(4):
                                tt = g2 * 4 + j
                                i = t_.transpose(ptpa[g2][:, j * 128:(j + 1) * 128],
                                                 vstage[vh % 2][:, tt * 128:(tt + 1) * 128],
                                                 identb[:, :])
                            p_tp.inc(i)

            @blk2.vector
            def _(v):
                for mc in range(8):
                    v.wait_ge(p_mm.h, b_pmmV + mc + 1)
                    if mc >= 2:
                        v.wait_ge(p_tp.h, b_tpV + 2 * (mc - 1))
                    EV.emit(v.tensor_copy(vstage[mc % 2][:, :], ppr[mc % 2][:, :]),
                            v_ev)
                    for g2 in range(2):
                        v.wait_ge(p_tp.h, b_tpV + 2 * mc + g2 + 1)
                        dst = bass.AP(vloc, (g2 * 4) * TLOC + mc * 128,
                                      [[8 * TLOC, 128], [TLOC, 4], [1, 128]])
                        src_ap = ptpa[g2][:, :].rearrange("p (j t) -> p j t", j=4)
                        EV.emit(v.tensor_copy(dst, src_ap), v_ev)

            @blk2.sync
            def _(sy):
                sy.wait_ge(v_ev.h, b_vevV + 24)
                for t in range(NT):
                    d_io.inc(sy.dma_start(
                        out=kv_loc[1024 + t * 128:1024 + (t + 1) * 128, :],
                        in_=vloc[:, t * TLOC:(t + 1) * TLOC]), 16)

            kv_store_cnt = d_io.n

            def _cc_hook(g, mc):
                if mc != 2:
                    return
                g.wait_ge(d_io.h, kv_store_cnt)
                DS.wait_all(g)
                s_cc.inc(g.collective_compute(
                    "AllGather", ALU.bypass,
                    replica_groups=[list(range(NCORES))],
                    ins=[kv_loc.ap().opt()],
                    outs=[kv_g.ap().opt()]))

            b_pmmQ = proj_loop(blk2, wq_t, 32, wbuf, ppr, hT, v_ev, 1,
                               g_hook=_cc_hook)

            @blk2.vector
            def _(v):
                for mc in range(32):
                    v.wait_ge(p_mm.h, b_pmmQ + mc + 1)
                    rope_post(v, ppr[mc % 2], QT, mc * TLOC, NQH * TLOC)

        if taps:
            with nc.Block() as blkt:
                @blkt.sync
                def _(sy):
                    d_io.inc(sy.dma_start(out=tp["QT"][:, :], in_=QT[:, :]), 16)
                    sy.wait_ge(s_cc.h, 1)
                    d_io.inc(sy.dma_start(out=tp["kvg"][:, :], in_=kv_g[:, :]), 16)
                    sy.wait_ge(d_io.h, d_io.n)

        PAIR_UNITS = []
        for B_ in range(NT):
            for b_ in range(8):
                if B_ < 4:
                    PAIR_UNITS.append((B_, b_, 128 * B_, 512))
                    PAIR_UNITS.append((B_, b_, 512, 1024))
                else:
                    PAIR_UNITS.append((B_, b_, 128 * B_, 1024))
        NUP = len(PAIR_UNITS)
        NTRI = NUP // 3
        BAND_END = [16 * (B_ + 1) if B_ < 4 else 64 + 8 * (B_ - 3)
                    for B_ in range(NT)]
        LFIRST, LLAST = {}, {}
        for idx, (B_, b_, a0_, a1_) in enumerate(PAIR_UNITS):
            key = (idx % 3, 0 if a1_ <= 512 else 1)
            LFIRST.setdefault(key, idx)
            LLAST[key] = idx
        MCNT = []
        _mc = 0
        for idx, (B_, b_, a0_, a1_) in enumerate(PAIR_UNITS):
            if a0_ == 128 * B_:
                _mc += 1
            MCNT.append(_mc)
        NMSK = _mc
        DEPTH = 4

        def glm(w):
            return (w // NUP) * NTRI + (w % NUP) // 3 + 1

        if lim < 3:
            return nc
        with ExitStack() as ph:
            pe = ph.enter_context
            kbuf = [pe(nc.sbuf_tensor(f"kb{i}", [128, 8192], BF16)) for i in range(2)]
            vbuf = [pe(nc.sbuf_tensor(f"vb{i}", [128, TLOC], BF16)) for i in range(4)]
            ptb = [pe(nc.sbuf_tensor(f"ptb{i}", [128, 512], BF16)) for i in range(8)]
            lrec = pe(nc.sbuf_tensor("lrec", [1, TLOC], F32))
            ls32 = pe(nc.sbuf_tensor("ls32", [1, TLOC], F32))
            ls64 = pe(nc.sbuf_tensor("ls64", [1, TLOC], F32))
            bcs = [pe(nc.sbuf_tensor(f"bcs{i}", [128, TLOC], F32)) for i in range(2)]
            osb = [pe(nc.sbuf_tensor(f"osb{i}", [128, TLOC], F32)) for i in range(2)]
            pst = [pe(nc.psum_tensor(f"pst{i}", [128, 512], F32)) for i in range(4)]
            pot = pe(nc.psum_tensor("pot", [128, TLOC], F32))
            pl = pe(nc.psum_tensor("pl", [128, TLOC], F32))
            blk2 = pe(nc.Block())

            b_kh = DK.base()
            b_dv4 = [sx.n for sx in DV4]
            b_sc, b_pv, b_ms, b_ex = p_sc.n, p_pv.n, v_ms.n, a_ex.n
            b_oe, b_oc, b_lr = v_oe.n, v_oc.n, v_lr.n
            b_lm, b_alr = p_lm.n, a_lr.n
            b_db1, b_db2 = d_b1.n, d_b2.n

            def uend(bi):
                return (bi // NT) * NUP + BAND_END[bi % NT]

            @blk2.sync
            def _(sy):
                sy.wait_ge(s_cc.h, 1)
                for h in range(NKV):
                    if h >= 2:
                        sy.wait_ge(p_sc.h, b_sc + (h - 1) * NUP * GQ)
                    for r in range(8):
                        src = bass.AP(kv_g, (r * 2048 + h * 128) * 1024,
                                      [[1024, 128], [1, 1024]])
                        DK.inc(sy.dma_start(out=kbuf[h % 2][:, r * 1024:(r + 1) * 1024],
                                            in_=src), h)
                    for qh in range(GQ):
                        for B in range(NT):
                            bi = (h * GQ + qh) * NT + B
                            if bi % 2 != 0:
                                continue
                            if bi >= 4:
                                sy.wait_ge(p_pv.h, b_pv + uend(bi - 4))
                            dst = vbuf[bi % 4][:, :].rearrange("p (b d) -> p b d", b=8)
                            src = bass.AP(kv_g, (1024 + B * 128) * 1024 + h * 128,
                                          [[1024, 128], [2048 * 1024, 8], [1, 128]])
                            DV4[bi % 4].inc(sy.dma_start(out=dst, in_=src), 16)

            @blk2.gpsimd
            def _(g):
                g.wait_ge(s_cc.h, 1)

                def bcast(pair):
                    g.wait_ge(a_lr.h, b_alr + pair + 1)
                    d_b1.inc(g.dma_start(out=lrow_d.ap(), in_=lrec[0:1, :]), 16)
                    g.wait_ge(d_b1.h, d_b1.n)
                    d_b2.inc(g.dma_start(
                        out=bcs[pair % 2][:, :],
                        in_=bass.AP(lrow_d, 0, [[0, 128], [1, TLOC]])), 16)

                for h in range(NKV):
                    for qh in range(GQ):
                        pair = h * GQ + qh
                        for B in range(NT):
                            bi = pair * NT + B
                            if bi % 2 != 1:
                                continue
                            if bi >= 4:
                                g.wait_ge(p_pv.h, b_pv + uend(bi - 4))
                            dst = vbuf[bi % 4][:, :].rearrange("p (b d) -> p b d", b=8)
                            src = bass.AP(kv_g, (1024 + B * 128) * 1024 + h * 128,
                                          [[1024, 128], [2048 * 1024, 8], [1, 128]])
                            DV4[bi % 4].inc(g.dma_start(out=dst, in_=src), 16)
                        if pair >= 1:
                            bcast(pair - 1)
                bcast(NKV * GQ - 1)

            @blk2.scalar
            def _(a):
                for h in range(NKV):
                    for qh in range(GQ):
                        pair = h * GQ + qh
                        for idx, (B, b, a0, a1) in enumerate(PAIR_UNITS):
                            u = pair * NUP + idx
                            ln_ = a1 - a0
                            a.wait_ge(p_sc.h, b_sc + u + 1)
                            if u >= 8:
                                a.wait_ge(p_pv.h, b_pv + u - 7)
                                a.wait_ge(p_lm.h, b_lm + glm(u - 8))
                            EA.emit(a.activation(
                                ptb[u % 8][:, 0:ln_], pst[u % 4][:, 0:ln_], AF.Exp,
                                scale=consts[:, C_INVSQ:C_INVSQ + 1],
                                bias=consts[:, C_NEGM:C_NEGM + 1]), a_ex)
                        a.wait_ge(v_lr.h, b_lr + pair + 1)
                        EA.emit(a.activation(ls64[0:1, :], ls32[0:1, :], AF.Ln))
                        abar(a)
                        a.wait_ge(d_b1.h, b_db1 + 16 * pair)
                        EA.emit(a.activation(lrec[0:1, :], ls64[0:1, :], AF.Exp,
                                             scale=consts[0:1, C_NEG1:C_NEG1 + 1]),
                                a_lr)

            @blk2.vector
            def _(v):
                for h in range(NKV):
                    for qh in range(GQ):
                        pair = h * GQ + qh
                        for idx, (B, b, a0, a1) in enumerate(PAIR_UNITS):
                            if a0 != 128 * B:
                                continue
                            u = pair * NUP + idx
                            v.wait_ge(a_ex.h, b_ex + u + 1)
                            EV.emit(v.tensor_tensor(
                                ptb[u % 8][:, 0:128], ptb[u % 8][:, 0:128],
                                maskb[:, b * 128:(b + 1) * 128], ALU.mult), v_ms)
                        v.wait_ge(p_pv.h, b_pv + (pair + 1) * NUP)
                        vbar(v)
                        EV.emit(v.tensor_copy(osb[pair % 2][:, :], pot[:, :]), v_oc)
                        v.wait_ge(p_lm.h, b_lm + (pair + 1) * NTRI)
                        v.wait_ge(a_lr.h, b_alr + pair)
                        EV.emit(v.tensor_copy(ls32[0:1, :], pl[32:33, 0:TLOC]))
                        EV.emit(v.tensor_copy(ls64[0:1, :], pl[64:65, 0:TLOC]))
                        vbar(v)
                        EV.emit(v.tensor_tensor(ls32[0:1, :], pl[0:1, 0:TLOC],
                                                ls32[0:1, :], ALU.add))
                        vbar(v)
                        EV.emit(v.tensor_tensor(ls32[0:1, :], ls32[0:1, :],
                                                ls64[0:1, :], ALU.add), v_lr)
                        if pair >= 1:
                            pv_ = pair - 1
                            v.wait_ge(d_b2.h, b_db2 + 16 * (pv_ + 1))
                            vbar(v)
                            EV.emit(v.tensor_tensor(
                                OT[:, pv_ * TLOC:(pv_ + 1) * TLOC],
                                osb[pv_ % 2][:, :], bcs[pv_ % 2][:, :],
                                ALU.mult), v_oe)
                pv_ = NKV * GQ - 1
                v.wait_ge(d_b2.h, b_db2 + 16 * (pv_ + 1))
                vbar(v)
                EV.emit(v.tensor_tensor(OT[:, pv_ * TLOC:(pv_ + 1) * TLOC],
                                        osb[pv_ % 2][:, :], bcs[pv_ % 2][:, :],
                                        ALU.mult), v_oe)

            @blk2.tensor
            def _(t_):
                from collections import deque
                pend = deque()
                lpend = []

                def emit_pv_unit(u, pair, B, b, a0, a1, idx, bi):
                    if a0 == 128 * B:
                        t_.wait_ge(v_ms.h, b_ms + pair * NMSK + MCNT[idx])
                    else:
                        t_.wait_ge(a_ex.h, b_ex + u + 1)
                    if idx == 0:
                        t_.wait_ge(v_oc.h, b_oc + pair)
                    first = (B == 0 and b == 0)
                    if a1 <= 512:
                        sp = (B == 3 and b == 7)
                    else:
                        sp = (B == 7 and b == 7)
                    i = t_.matmul(pot[:, a0:a1],
                                  vbuf[bi % 4][:, b * 128:(b + 1) * 128],
                                  ptb[u % 8][:, 0:a1 - a0], start=first, stop=sp)
                    p_pv.inc(i)

                def emit_l_triple():
                    for (u, pair, B, b, a0, a1, idx, bi) in lpend:
                        if a0 == 128 * B:
                            t_.wait_ge(v_ms.h, b_ms + pair * NMSK + MCNT[idx])
                    umax = max(info[0] for info in lpend)
                    t_.wait_ge(a_ex.h, b_ex + umax + 1)
                    if lpend[0][6] == 0:
                        t_.wait_ge(v_lr.h, b_lr + lpend[0][1])
                    i = None
                    for (u, pair, B, b, a0, a1, idx, bi) in lpend:
                        row = idx % 3
                        bank = 0 if a1 <= 512 else 1
                        st = (LFIRST[(row, bank)] == idx)
                        sp = (LLAST[(row, bank)] == idx)
                        i = t_.matmul(pl[32 * row:32 * row + 1, a0:a1],
                                      ones_col[:, :], ptb[u % 8][:, 0:a1 - a0],
                                      start=st, stop=sp,
                                      tile_position=(0, 32 * row))
                    p_lm.inc(i)
                    lpend.clear()

                def flush_one():
                    info = pend.popleft()
                    emit_pv_unit(*info)
                    lpend.append(info)
                    if len(lpend) == 3:
                        emit_l_triple()

                for h in range(NKV):
                    DK.wait(t_, b_kh, h, per=128)
                    for qh in range(GQ):
                        pair = h * GQ + qh
                        for idx, (B, b, a0, a1) in enumerate(PAIR_UNITS):
                            u = pair * NUP + idx
                            bi = pair * NT + B
                            if u >= 4:
                                t_.wait_ge(a_ex.h, b_ex + u - 3)
                            if b == 0 and a0 == 128 * B:
                                t_.wait_ge(DV4[bi % 4].h,
                                           b_dv4[bi % 4] + 16 * (bi // 4 + 1))
                            p_sc.inc(t_.matmul(
                                pst[u % 4][:, 0:a1 - a0],
                                kbuf[h % 2][:, (b * 8 + B) * 128:
                                            (b * 8 + B + 1) * 128],
                                QT[:, pair * TLOC + a0:pair * TLOC + a1],
                                start=True, stop=True))
                            pend.append((u, pair, B, b, a0, a1, idx, bi))
                            if len(pend) > DEPTH:
                                flush_one()
                    while pend:
                        flush_one()

        if taps:
            with nc.Block() as blkt:
                @blkt.sync
                def _(sy):
                    d_io.inc(sy.dma_start(out=tp["OT"][:, :], in_=OT[:, :]), 16)
                    sy.wait_ge(d_io.h, d_io.n)

        def out_proj_phase(wdram, rhs_sb, bias_col, with_lora, res_src, tag,
                           level=4):
            with ExitStack() as ph:
                pe = ph.enter_context
                wbuf2 = [pe(nc.sbuf_tensor(f"wc{tag}{i}", [128, NCH * 128], BF16))
                         for i in range(2)]
                pw = [pe(nc.psum_tensor(f"pw{tag}{i}", [128, TLOC], F32)) for i in range(2)]
                ptpf = [pe(nc.psum_tensor(f"ptpf{tag}{i}", [128, 512], F32))
                        for i in range(2)]
                aT = [pe(nc.sbuf_tensor(f"aT{tag}{i}", [128, TLOC], F32)) for i in range(2)]
                atm = [pe(nc.sbuf_tensor(f"atm{tag}{i}", [128, TLOC], F32))
                       for i in range(2)]
                xres = [pe(nc.sbuf_tensor(f"xr{tag}{i}", [128, TLOC], F32))
                        for i in range(3)]
                blk2 = pe(nc.Block())

                b_dw, b_pmm, b_ah, b_ptp, b_vev, b_ds, b_ad = (
                    DW.base(), p_mm.n, a_h.n, p_tp.n, v_ev.n, DS.base(),
                    v_ad.n)
                b_dx3 = [sx.n for sx in DX3]

                @blk2.gpsimd
                def _(g):
                    for oc in range(32):
                        if oc >= 2:
                            g.wait_ge(p_mm.h, b_pmm + oc - 1)
                        src = bass.AP(wdram, oc * 128 * NCH * 128,
                                      [[NCH * 128, 128], [1, NCH * 128]])
                        DW.inc(g.dma_start(out=wbuf2[oc % 2][:, :], in_=src), oc)

                def issue_xres(q, oc):
                    srcx = res_src[:, oc * 128:(oc + 1) * 128].rearrange(
                        "(a p) o -> p a o", p=128)
                    DX3[oc % 3].inc(q.dma_start(
                        out=xres[oc % 3][:, :].rearrange(
                            "p (a o) -> p a o", a=8), in_=srcx), 16)

                if level >= 4:
                    @blk2.sync
                    def _(sy):
                        for so in range(32):
                            sy.wait_ge(v_ad.h, b_ad + so + 1)
                            dsty = y[:, so * 128:(so + 1) * 128].rearrange(
                                "(a p) o -> p a o", p=128)
                            srcy = atm[so % 2][:, :].rearrange(
                                "p (a o) -> p a o", a=8)
                            DS.inc(sy.dma_start(out=dsty, in_=srcy), so)

                @blk2.tensor
                def _(t_):
                    t_.wait_ge(v_oe.h, v_oe.n)
                    for oc in range(33):
                        if oc < 32:
                            DW.wait(t_, b_dw, oc)
                            if oc >= 2 and level >= 2:
                                t_.wait_ge(a_h.h, b_ah + oc - 1)
                            i = None
                            last_plain = not with_lora
                            for ic in range(NCH):
                                for sg in range(2):
                                    i = t_.matmul(
                                        pw[oc % 2][:, sg * 512:(sg + 1) * 512],
                                        wbuf2[oc % 2][:, ic * 128:(ic + 1) * 128],
                                        rhs_sb[:, ic * TLOC + sg * 512:
                                               ic * TLOC + (sg + 1) * 512],
                                        start=(ic == 0),
                                        stop=(last_plain and ic == NCH - 1))
                            if with_lora:
                                for sg in range(2):
                                    i = t_.matmul(
                                        pw[oc % 2][:, sg * 512:(sg + 1) * 512],
                                        bmt_bf[:, oc * 128:(oc + 1) * 128],
                                        grt_sb[:, sg * 512:(sg + 1) * 512],
                                        start=False, stop=True)
                            p_mm.inc(i)
                        if oc >= 1 and level >= 3:
                            po = oc - 1
                            t_.wait_ge(a_h.h, b_ah + po + 1)
                            for g2 in range(2):
                                if po * 2 + g2 >= 2 and level >= 3:
                                    t_.wait_ge(v_ev.h, b_vev + 2 * po + g2 - 1)
                                i = None
                                for j in range(4):
                                    tt = g2 * 4 + j
                                    i = t_.transpose(
                                        ptpf[g2][:, j * 128:(j + 1) * 128],
                                        aT[po % 2][:, tt * 128:(tt + 1) * 128],
                                        identf[:, :])
                                p_tp.inc(i)

                @blk2.scalar
                def _(a):
                    if level < 2:
                        return
                    if level >= 4:
                        for si in range(2):
                            a.wait_ge(DS.s[si].h, b_ds[si])
                        for j in range(3):
                            issue_xres(a, j)
                    for oc in range(32):
                        a.wait_ge(p_mm.h, b_pmm + oc + 1)
                        if oc >= 2 and level >= 3:
                            a.wait_ge(p_tp.h, b_ptp + 2 * (oc - 1))
                        if bias_col is not None:
                            EA.emit(a.activation(aT[oc % 2][:, :], pw[oc % 2][:, :],
                                                 AF.Identity,
                                                 bias=bias_col[:, oc:oc + 1]), a_h)
                        else:
                            EA.emit(a.activation(aT[oc % 2][:, :], pw[oc % 2][:, :],
                                                 AF.Identity), a_h)
                        if level >= 4 and oc + 3 < 32:
                            a.wait_ge(v_ad.h, b_ad + oc + 1)
                            issue_xres(a, oc + 3)

                @blk2.vector
                def _(v):
                    if level < 3:
                        return
                    for po in range(32):
                        for g2 in range(2):
                            v.wait_ge(p_tp.h, b_ptp + 2 * po + g2 + 1)
                            if po >= 2 and level >= 4:
                                DS.wait(v, b_ds, po - 2)
                            dst = bass.AP(atm[po % 2], (g2 * 4) * 128,
                                          [[TLOC, 128], [128, 4], [1, 128]])
                            src_ap = ptpf[g2][:, :].rearrange("p (j t) -> p j t", j=4)
                            EV.emit(v.tensor_copy(dst, src_ap), v_ev)
                        if level >= 4:
                            v.wait_ge(DX3[po % 3].h,
                                      b_dx3[po % 3] + 16 * (po // 3 + 1))
                            vbar(v)
                            EV.emit(v.tensor_tensor(atm[po % 2][:, :],
                                                    atm[po % 2][:, :],
                                                    xres[po % 3][:, :], ALU.add),
                                    v_ad)

        if lim < 4:
            return nc
        out_proj_phase(wo_t, OT, None, False, x_sh, 'o', level=WO_LEVEL)


        with nc.Block() as blkw:
            @blkw.sync
            def _(sy):
                DS.wait_all(sy)
                if taps:
                    d_io.inc(sy.dma_start(out=tp["x1"][:, :], in_=y[:, :]), 16)
                    sy.wait_ge(d_io.h, d_io.n)

        if lim < 5:
            return nc
        hT2 = QT
        ln_phase(y, hT2, 'c')

        if taps:
            with nc.Block() as blkt:
                @blkt.sync
                def _(sy):
                    d_io.inc(sy.dma_start(out=tp["h2T"][:, :], in_=hT2[:, :]), 16)
                    sy.wait_ge(d_io.h, d_io.n)

        if lim < 6:
            return nc
        bmt_bf = st.enter_context(nc.sbuf_tensor("bmt_bf", [128, D], BF16))
        grt_sb = st.enter_context(nc.sbuf_tensor("grt_sb", [128, TLOC], BF16))
        with ExitStack() as ph:
            pe = ph.enter_context
            awr_bf = pe(nc.sbuf_tensor("awr_bf", [128, NCH * 36], BF16))
            grp = pe(nc.sbuf_tensor("grp", [128, 128], BF16))
            gex = pe(nc.sbuf_tensor("gex", [128, 4], F32))
            gfin = pe(nc.sbuf_tensor("gfin", [128, 4], F32))
            pr = pe(nc.psum_tensor("pr", [128, 512], F32))
            pgt = pe(nc.psum_tensor("pgt", [128, 128], BF16))
            blk2 = pe(nc.Block())

            b_dw, b_pmm, b_gr, b_tp, b_aex = DW.base(), p_mm.n, v_gr.n, p_tp.n, a_ex.n

            @blk2.gpsimd
            def _(g):
                DW.inc(g.dma_start(out=awr_bf[:, :],
                                    in_=awr[:, :, :].rearrange("p a b -> p (a b)")), 0)
                DW.inc(g.dma_start(out=bmt_bf[:, :], in_=bmt[:, :]), 1)

            @blk2.tensor
            def _(t_):
                DW.wait(t_, b_dw, 0)
                DW.wait(t_, b_dw, 1)
                for t in range(NT):
                    if t >= 1:
                        t_.wait_ge(v_gr.h, b_gr + 2 * t)
                    i = None
                    for ic in range(NCH):
                        i = t_.matmul(
                            pr[:, 0:36],
                            hT2[:, ic * TLOC + t * 128:ic * TLOC + (t + 1) * 128],
                            awr_bf[:, ic * 36:(ic + 1) * 36],
                            start=(ic == 0), stop=(ic == NCH - 1))
                    p_mm.inc(i)
                    t_.wait_ge(v_gr.h, b_gr + 2 * t + 2)
                    p_tp.inc(t_.transpose(pgt[:, :], grp[:, :], identb[:, :]))

            @blk2.vector
            def _(v):
                EV.emit(v.memset(grp[:, 32:128], 0.0))
                for t in range(NT):
                    v.wait_ge(p_mm.h, b_pmm + t + 1)
                    vbar(v)
                    EV.emit(v.reduce_max(stt[:, NTMP:NTMP + 1], pr[:, 32:36], AX.X))
                    vbar(v)
                    EV.emit(v.tensor_scalar_mul(stt[:, NTMP:NTMP + 1],
                                                stt[:, NTMP:NTMP + 1], -1.0), v_gr)
                    v.wait_ge(a_ex.h, b_aex + t + 1)
                    EV.emit(v.reduce_sum(stt[:, SUM:SUM + 1], gex[:, :], AX.X))
                    vbar(v)
                    EV.emit(v.reciprocal(stt[:, SUM:SUM + 1], stt[:, SUM:SUM + 1]))
                    vbar(v)
                    EV.emit(v.tensor_scalar(gfin[:, :], gex[:, :],
                                            stt[:, SUM:SUM + 1], 2.0,
                                            ALU.mult, ALU.mult))
                    vbar(v)
                    for ex in range(3):
                        EV.emit(v.tensor_scalar(grp[:, ex * 8:(ex + 1) * 8],
                                                pr[:, ex * 8:(ex + 1) * 8],
                                                gfin[:, ex:ex + 1], 0.0,
                                                ALU.mult, ALU.add))
                    EV.emit(v.tensor_scalar(grp[:, 24:32], pr[:, 24:32],
                                            gfin[:, 3:4], 0.0,
                                            ALU.mult, ALU.add), v_gr)
                    v.wait_ge(p_tp.h, b_tp + t + 1)
                    EV.emit(v.tensor_copy(grt_sb[:, t * 128:(t + 1) * 128],
                                          pgt[:, :]), v_ev)

            @blk2.scalar
            def _(a):
                for t in range(NT):
                    a.wait_ge(v_gr.h, b_gr + 2 * t + 1)
                    EA.emit(a.activation(gex[:, :], pr[:, 32:36], AF.Exp,
                                         bias=stt[:, NTMP:NTMP + 1]), a_ex)

        if lim < 7:
            return nc
        out_proj_phase(wt_t, hT2, bT, True, y, 'f')

        with nc.Block() as blkf:
            @blkf.gpsimd
            def _(g):
                DS.wait_all(g)

            @blkf.sync
            def _(sy):
                DS.wait_all(sy)

    return nc


def _host_inputs(L0_attn_X, Wq, Wk, Wv, Wo, W, b, A, Bm, Wr):
    X = np.ascontiguousarray(np.asarray(L0_attn_X, dtype=np.float32).reshape(8192, D))
    Wq = np.asarray(Wq, np.float32)
    Wk = np.asarray(Wk, np.float32)
    Wv = np.asarray(Wv, np.float32)
    Wo = np.asarray(Wo, np.float32)
    W = np.asarray(W, np.float32)
    b = np.asarray(b, np.float32)
    A = np.asarray(A, np.float32)
    Bm = np.asarray(Bm, np.float32)
    Wr = np.asarray(Wr, np.float32)

    def tile_w(M):
        K, Mo = M.shape
        return np.ascontiguousarray(
            M.reshape(K // 128, 128, Mo // 128, 128).transpose(2, 1, 0, 3))

    AWr = np.concatenate([A.reshape(32, D).T, Wr.T], axis=1)
    awr = np.ascontiguousarray(AWr.reshape(NCH, 128, 36).transpose(1, 0, 2))
    bmt = np.zeros((128, D), np.float32)
    bmt[0:32] = Bm.transpose(0, 2, 1).reshape(32, D)
    b_t = np.ascontiguousarray(b.reshape(32, 128).T)

    consts = np.tile(np.asarray(CONSTS, np.float32)[None, :], (128, 1))
    ident = np.eye(128, dtype=np.float32)

    shared = dict(wq_t=tile_w(Wq), wk_t=tile_w(Wk), wv_t=tile_w(Wv),
                  wo_t=tile_w(Wo), wt_t=tile_w(np.ascontiguousarray(W.T)),
                  awr=awr, bmt=bmt, b_t=b_t, consts_in=consts, ident_in=ident)

    Xr = X.reshape(64, 128, D)
    inv_freq = (10000.0 ** (-np.arange(64, dtype=np.float64) / 64.0))
    p_idx = np.arange(128)
    t_idx = np.arange(128)
    in_maps = []
    for c in range(NCORES):
        m = dict(shared)
        m["x_sh"] = np.ascontiguousarray(Xr[c::8].reshape(TLOC, D))
        pos = (128 * (c + 8 * np.arange(8))[:, None]
               + np.arange(128)[None, :]).reshape(-1)
        ang = inv_freq[:, None] * pos[None, :].astype(np.float64)
        m["cos_in"] = np.cos(ang).astype(np.float32)
        m["sin_in"] = np.sin(ang).astype(np.float32)
        mask = np.zeros((128, 8 * 128), np.float32)
        for bb in range(8):
            diff = p_idx[:, None] - t_idx[None, :]
            mask[:, bb * 128:(bb + 1) * 128] = np.where(
                diff > 128 * (c - bb), 0.0, 1.0)
        m["mask_in"] = mask
        in_maps.append(m)
    return in_maps


def kernel(L0_attn_X, L0_attn_Wq, L0_attn_Wk, L0_attn_Wv, L0_attn_Wo,
           L0_ffn_W, L0_ffn_b, L0_ffn_A, L0_ffn_Bm, L0_ffn_Wr):
    in_maps = _host_inputs(L0_attn_X, L0_attn_Wq, L0_attn_Wk, L0_attn_Wv,
                           L0_attn_Wo, L0_ffn_W, L0_ffn_b, L0_ffn_A,
                           L0_ffn_Bm, L0_ffn_Wr)
    nc = build_nc(taps=False)
    res = run_bass_kernel_spmd(nc, in_maps, core_ids=list(range(NCORES)))
    out = np.empty((64, 128, D), np.float32)
    for c in range(NCORES):
        out[c::8] = res.results[c]["y"].reshape(NT, 128, D)
    return out.reshape(1, 8192, D)
